# version 03
# baseline (speedup 1.0000x reference)
import sys
sys.path.insert(0, "/opt/trn_rl_repo")
import numpy as np
import ml_dtypes
from contextlib import ExitStack
import concourse.bass as bass
import concourse.mybir as mybir
from concourse.bass_utils import run_bass_kernel_spmd

F32 = mybir.dt.float32
F32R = mybir.dt.float32r
BF16 = mybir.dt.bfloat16
AF = mybir.ActivationFunctionType
ALU = mybir.AluOpType
AX = mybir.AxisListType

NCORE = 8
T = 4096
TT = 512
NT = T // TT
D = 1024
NCH = D // 128
EPS = 1e-5
BF = np.dtype(ml_dtypes.bfloat16)


class Sem:
    __slots__ = ("h", "val")

    def __init__(self, h):
        self.h = h
        self.val = 0

    def inc(self, n=1):
        self.val += n
        return self.val


class Ctx:
    ENGINES = ("tensor", "vector", "scalar", "sync", "gpsimd")

    def __init__(self, nc):
        self.nc = nc
        self.es = ExitStack()
        self.ops = {e: [] for e in self.ENGINES}
        self.sems = {}
        self.dram = {}
        self.consts_ready = None

    def sem(self, name):
        if name not in self.sems:
            h = self.es.enter_context(self.nc.semaphore(name))
            self.sems[name] = Sem(h)
        return self.sems[name]

    phase_tag = ""

    def sbuf(self, name, shape, dtype, scope=None):
        return (scope or self.es).enter_context(
            self.nc.sbuf_tensor(self.phase_tag + name, shape, dtype))

    def psum(self, name, shape, dtype, scope=None):
        return (scope or self.es).enter_context(
            self.nc.psum_tensor(self.phase_tag + name, shape, dtype))

    def emit(self, eng, fn):
        self.ops[eng].append(fn)

    def wait(self, eng, sem, tgt):
        if tgt is None or tgt <= 0:
            return

        def _w(x, s=sem.h, tgt=tgt):
            x.wait_ge(s, tgt)
        self.emit(eng, _w)

    def barrier(self):
        bs = self.sem("BAR")
        if not hasattr(self, "_fence"):
            self._fence = self.sbuf("fence_t", [1, 8], F32)
        base = bs.val
        bs.inc(5)
        ft = self._fence
        for e in self.ENGINES:
            if e in ("vector", "gpsimd"):
                def _inc(x, s=bs.h, ft=ft):
                    x.memset(ft[0:1, 0:1], 0.0).then_inc(s, 1)
            elif e == "scalar":
                def _inc(x, s=bs.h, ft=ft):
                    x.copy(ft[0:1, 1:2], ft[0:1, 0:1]).then_inc(s, 1)
            else:
                def _inc(x, s=bs.h):
                    x.nop().then_inc(s, 1)
            self.emit(e, _inc)
        for e in self.ENGINES:
            def _wt(x, s=bs.h, tgt=base + 5):
                x.wait_ge(s, tgt)
            self.emit(e, _wt)

    def finalize(self):
        ops = self.ops
        with self.nc.Block() as block:
            @block.tensor
            def _(t):
                for fn in ops["tensor"]:
                    fn(t)

            @block.vector
            def _(v):
                for fn in ops["vector"]:
                    fn(v)

            @block.scalar
            def _(s):
                for fn in ops["scalar"]:
                    fn(s)

            @block.sync
            def _(sy):
                for fn in ops["sync"]:
                    fn(sy)

            @block.gpsimd
            def _(g):
                for fn in ops["gpsimd"]:
                    fn(g)
        self.es.close()


WEIGHT_SPECS = {
    "qwT": ([D, D], F32), "kkwT": ([D, 128], F32), "vwT": ([D, 64], F32),
    "projT": ([D, D], F32), "proj_b": ([D], F32),
    "g1R": ([128, 32], F32), "g2R": ([16, 128, 512], F32),
    "g1I": ([128, 32], F32), "g2I": ([16, 128, 512], F32),
    "l0f_br": ([512], F32), "l0f_bi": ([512], F32),
    "inpT": ([D, 2 * D], F32),
    "convw0": ([D], F32), "convw1": ([D], F32), "convb": ([D], F32),
    "xprojT": ([D, 128], F32),
    "dtprojT": ([64, D], F32), "dtprojb": ([D], F32),
    "Alog": ([D, 16], F32), "Dp": ([D], F32),
    "outprojT": ([D, D], F32),
    "mnorm_g": ([D], F32), "mnorm_b": ([D], F32),
    "n1_g": ([D], F32), "n1_b": ([D], F32),
    "n2_g": ([D], F32), "n2_b": ([D], F32),
    "n3_g": ([D], F32), "n3_b": ([D], F32),
    "ffw1T": ([D, 4 * D], F32), "ffw2T": ([4 * D, D], F32),
    "feT": ([D, D], F32), "feb": ([D], F32),
    "gwT": ([D, 32], F32), "gbias": ([32], F32),
    "wgT": ([32, D, 512], F32), "wuT": ([32, D, 512], F32),
    "wdT": ([32, 512, D], F32),
    "shgT": ([D, D], F32), "shuT": ([D, D], F32), "shdT": ([D, D], F32),
}

SCRATCH_SPECS = {
    "dbg_sc": ([128, 256], F32),
    "dbg_pn": ([128, 256], BF16),
    "dbg_pn2": ([128, 256], BF16),
    "dbg_pt": ([128, 256], BF16),
    "dbg_sm": ([128, 4], F32),
    "dbg_av": ([64, 128], F32),
    "dbg_q": ([D, T], BF16),
    "dbg_ao": ([D, T], BF16),
    "dbg_kk": ([128, T], BF16),
    "dbg_v": ([64, T], BF16),
    "xn0": ([D, T], BF16),
    "att": ([D, T], F32),
    "x1": ([D, T], F32),
    "xnL1": ([D, T], F32),
    "xnL1b": ([D, T], BF16),
    "s1": ([D, T], BF16),
    "xc32": ([D, T], F32),
    "xcb": ([D, T], BF16),
    "sz": ([D, T], F32),
    "delta": ([D, T], F32),
    "Bm": ([16, T], F32),
    "Cm": ([16, T], F32),
    "m_out": ([D, T], F32),
    "src2": ([D, T], F32),
    "s2": ([D, T], BF16),
    "f1": ([4 * D, T], BF16),
    "src3": ([D, T], F32),
    "s3": ([D, T], BF16),
    "a_out": ([D, T], F32),
    "cT": ([32, T], F32),
    "base": ([D, T], F32),
    "wtR": ([512, 512], F32),
    "wtI": ([512, 512], F32),
}


def host_prep(inp):
    w = {}
    w["qwT"] = np.ascontiguousarray(inp["L0_attn_q_w"].T)
    kvwT = np.ascontiguousarray(inp["L0_attn_kv_w"].T)
    w["kkwT"] = np.ascontiguousarray(np.concatenate([kvwT[:, :64], kvwT[:, :64]], axis=1))
    w["vwT"] = np.ascontiguousarray(kvwT[:, 64:128])
    w["projT"] = np.ascontiguousarray(inp["L0_attn_proj_w"].T)
    w["proj_b"] = np.ascontiguousarray(inp["L0_attn_proj_b"])

    def tt_ops(G1, G2):
        lhsT = np.zeros((128, 32), np.float32)
        for c in range(4):
            lhsT[c * 32:(c + 1) * 32, :] = G1[0, :, c, :]
        rhs = np.zeros((16, 128, 512), np.float32)
        for g in range(16):
            for c in range(4):
                for b in range(32):
                    rhs[g, c * 32 + b, b * 16:(b + 1) * 16] = G2[c, :, 0, g]
        return lhsT, rhs
    w["g1R"], w["g2R"] = tt_ops(np.asarray(inp["L0_ffn_G1_R"]), np.asarray(inp["L0_ffn_G2_R"]))
    w["g1I"], w["g2I"] = tt_ops(np.asarray(inp["L0_ffn_G1_I"]), np.asarray(inp["L0_ffn_G2_I"]))
    w["l0f_br"] = np.ascontiguousarray(inp["L0_ffn_bias_real"])
    w["l0f_bi"] = np.ascontiguousarray(inp["L0_ffn_bias_imag"])

    w["inpT"] = np.ascontiguousarray(inp["L1_attn_in_proj_w"].T)
    cw = np.asarray(inp["L1_attn_conv_w"])
    w["convw0"] = np.ascontiguousarray(cw[:, 0])
    w["convw1"] = np.ascontiguousarray(cw[:, 1])
    w["convb"] = np.ascontiguousarray(inp["L1_attn_conv_b"])
    xp = np.asarray(inp["L1_attn_x_proj_w"])
    xppad = np.zeros((128, D), np.float32)
    xppad[0:64] = xp[0:64]
    xppad[64:80] = xp[64:80]
    xppad[96:112] = xp[80:96]
    w["xprojT"] = np.ascontiguousarray(xppad.T)
    w["dtprojT"] = np.ascontiguousarray(inp["L1_attn_dt_proj_w"].T)
    w["dtprojb"] = np.ascontiguousarray(inp["L1_attn_dt_proj_b"])
    w["Alog"] = np.ascontiguousarray(inp["L1_attn_A_log"])
    w["Dp"] = np.ascontiguousarray(inp["L1_attn_Dp"])
    w["outprojT"] = np.ascontiguousarray(inp["L1_attn_out_proj_w"].T)
    for k in ("mnorm_g", "mnorm_b", "n1_g", "n1_b", "n2_g", "n2_b", "n3_g", "n3_b"):
        w[k] = np.ascontiguousarray(inp["L1_attn_" + k])
    w["ffw1T"] = np.ascontiguousarray(inp["L1_attn_ffn_w1"].T)
    w["ffw2T"] = np.ascontiguousarray(inp["L1_attn_ffn_w2"].T)
    w["feT"] = np.ascontiguousarray(inp["L1_attn_fe_w"].T)
    w["feb"] = np.ascontiguousarray(inp["L1_attn_fe_b"])

    w["gwT"] = np.ascontiguousarray(inp["L1_ffn_gate_w"].T)
    w["gbias"] = np.ascontiguousarray(inp["L1_ffn_gate_bias"])
    w["wgT"] = np.ascontiguousarray(np.asarray(inp["L1_ffn_w_gate"]).transpose(0, 2, 1))
    w["wuT"] = np.ascontiguousarray(np.asarray(inp["L1_ffn_w_up"]).transpose(0, 2, 1))
    w["wdT"] = np.ascontiguousarray(np.asarray(inp["L1_ffn_w_down"]).transpose(0, 2, 1))
    w["shgT"] = np.ascontiguousarray(inp["L1_ffn_sh_gate"].T)
    w["shuT"] = np.ascontiguousarray(inp["L1_ffn_sh_up"].T)
    w["shdT"] = np.ascontiguousarray(inp["L1_ffn_sh_down"].T)

    x = np.asarray(inp["L0_attn_x"])
    xs = [np.ascontiguousarray(x[i].T) for i in range(NCORE)]
    for k in w:
        w[k] = np.asarray(w[k], dtype=np.float32)
    return w, xs


def emit_consts(ctx):
    if ctx.consts_ready is not None:
        return
    ctx.ones128b = ctx.sbuf("ones128b", [128, 1], BF16)
    ctx.ones1 = ctx.sbuf("ones1", [1, 128], F32)
    ctx.ones1b = ctx.sbuf("ones1b", [1, 128], BF16)
    ctx.identb = ctx.sbuf("identb", [128, 128], BF16)
    ctx.identf = ctx.sbuf("identf", [128, 128], F32)
    dv = ctx.sem("DV")
    tgt = dv.inc(1)

    def _mk(v, ctx=ctx, s=dv.h):
        v.memset(ctx.ones128b[:], 1.0)
        v.memset(ctx.ones1[:], 1.0)
        v.memset(ctx.ones1b[:], 1.0).then_inc(s, 1)
    ctx.emit("vector", _mk)
    gp = ctx.sem("GP")
    gtgt = gp.inc(1)

    def _mkid(g, ctx=ctx, s=gp.h):
        g.memset(ctx.identb[:], 0.0)
        g.affine_select(out=ctx.identb[:], in_=ctx.identb[:],
                        compare_op=ALU.not_equal, fill=1.0, base=0,
                        pattern=[[-1, 128]], channel_multiplier=1)
        g.memset(ctx.identf[:], 0.0)
        g.affine_select(out=ctx.identf[:], in_=ctx.identf[:],
                        compare_op=ALU.not_equal, fill=1.0, base=0,
                        pattern=[[-1, 128]], channel_multiplier=1).then_inc(s, 1)
    ctx.emit("gpsimd", _mkid)
    ctx.consts_ready = tgt
    ctx.ident_ready = gtgt


def emit_vec_loads(ctx, names, scope=None):
    ld = ctx.sem("VLD")
    tiles = {}
    chs = {}
    for nm in names:
        n = ctx.dram[nm].shape[0] // 128
        chs[nm] = n
        tiles[nm] = ctx.sbuf(nm + "_pv", [128, n], F32, scope)
    tgt = ld.inc(16 * sum(chs.values()))

    def _d(sy, items=[(tiles[nm], ctx.dram[nm], chs[nm]) for nm in names], s=ld.h):
        for dst, src, n in items:
            for c in range(n):
                sy.dma_start(out=dst[:, c:c + 1],
                             in_=src[c * 128:(c + 1) * 128, None]).then_inc(s, 16)
    ctx.emit("sync", _d)
    return tiles, tgt


def emit_ln(ctx, src_name, outs, affine=None, res_name=None):
    emit_consts(ctx)
    scope = ExitStack()
    src = ctx.dram[src_name]
    res = ctx.dram[res_name] if res_name else None
    LD = ctx.sem("LD")
    GL = ctx.sem("GL")
    PE = ctx.sem("PE")
    DV = ctx.sem("DV")
    AC = ctx.sem("AC")
    ST = ctx.sem("ST")

    xf = [[ctx.sbuf(f"ln_xf{r}_{c}", [128, TT], F32, scope) for c in range(NCH)]
          for r in range(2)]
    xb = [[ctx.sbuf(f"ln_xb{r}_{c}", [128, TT], BF16, scope) for c in range(NCH)]
          for r in range(2)]
    rs = [[ctx.sbuf(f"ln_rs{r}_{c}", [128, TT], F32, scope) for c in range(NCH)]
          for r in range(2)] if res is not None else None
    sq = [[ctx.sbuf(f"ln_sq{r}_{c}", [128, TT], BF16, scope) for c in range(NCH)]
          for r in range(2)]
    stat = [ctx.sbuf(f"ln_stat{r}", [1, 6 * TT], F32, scope) for r in range(2)]
    tmp = [ctx.sbuf(f"ln_tmp_{c}", [128, TT], F32, scope) for c in range(NCH)]
    tmp2 = [ctx.sbuf(f"ln_tmp2_{c}", [128, TT], F32, scope) for c in range(NCH)]
    ot = {}
    for (oname, odt) in outs:
        ot[oname] = [[ctx.sbuf(f"ln_o_{oname}{r}_{c}", [128, TT], odt, scope)
                      for c in range(NCH)] for r in range(2)]
    ps_s = [ctx.psum(f"ln_ps_s{r}", [1, TT], F32, scope) for r in range(2)]
    ps_q = [ctx.psum(f"ln_ps_q{r}", [1, TT], F32, scope) for r in range(2)]
    ps_a = [ctx.psum(f"ln_ps_a{r}", [128, TT], F32, scope) for r in range(2)]
    ps_b = [ctx.psum(f"ln_ps_b{r}", [128, TT], F32, scope) for r in range(2)]

    vld_tgt = None
    gt = bt = None
    if affine:
        vt, vld_tgt = emit_vec_loads(ctx, list(affine), scope)
        gt, bt = vt[affine[0]], vt[affine[1]]

    free_evt = [None, None]
    stats_evt = [None, None]
    apply_evt = [None, None]
    store_evt = [None, None]

    for i in range(NT):
        r = i % 2
        if free_evt[r] is not None:
            ctx.wait("sync", DV, free_evt[r])
            ctx.wait("gpsimd", DV, free_evt[r])
        nload = NCH * (3 if res is not None else 2)
        ld_tgt = LD.inc(16 * NCH * (2 if res is not None else 1))

        def _ld(sy, r=r, i=i, s=LD.h, has_res=res is not None):
            for c in range(NCH):
                sy.dma_start(out=xf[r][c][:],
                             in_=src[c * 128:(c + 1) * 128, i * TT:(i + 1) * TT]).then_inc(s, 16)
            if has_res:
                for c in range(NCH):
                    sy.dma_start(out=rs[r][c][:],
                                 in_=res[c * 128:(c + 1) * 128, i * TT:(i + 1) * TT]).then_inc(s, 16)
        ctx.emit("sync", _ld)
        gl_tgt = GL.inc(16 * NCH)

        def _ldb(g, r=r, i=i, s=GL.h):
            for c in range(NCH):
                g.dma_start(out=xb[r][c][:],
                            in_=src[c * 128:(c + 1) * 128, i * TT:(i + 1) * TT]).then_inc(s, 16)
        ctx.emit("gpsimd", _ldb)

        ctx.wait("scalar", GL, gl_tgt)
        if free_evt[r] is not None:
            pass
        ac_tgt = AC.inc(NCH)

        def _sq(s, r=r, sA=AC.h):
            for c in range(NCH):
                s.activation(sq[r][c][:], xb[r][c][:], AF.Square, bias=0.0,
                             scale=1.0).then_inc(sA, 1)
        ctx.emit("scalar", _sq)

        ctx.wait("tensor", AC, ac_tgt)
        ctx.wait("tensor", GL, gl_tgt)
        ctx.wait("tensor", DV, ctx.consts_ready)
        if stats_evt[r] is not None:
            ctx.wait("tensor", DV, stats_evt[r])
        pe_tgt = PE.inc(1)

        def _mm(t, r=r, s=PE.h, ctx=ctx):
            for c in range(NCH):
                t.matmul(ps_s[r][:], ctx.ones128b[:], xb[r][c][:],
                         start=(c == 0), stop=(c == NCH - 1))
            for c in range(NCH):
                m = t.matmul(ps_q[r][:], ctx.ones128b[:], sq[r][c][:],
                             start=(c == 0), stop=(c == NCH - 1))
            m.then_inc(s, 1)
        ctx.emit("tensor", _mm)

        ctx.wait("vector", PE, pe_tgt)
        st1 = DV.inc(1)

        def _st1(v, r=r, s=DV.h):
            S = stat[r]
            v.tensor_scalar(S[:, 0:TT], ps_s[r][:], 1.0 / D, None, ALU.mult)
            v.tensor_scalar(S[:, TT:2 * TT], ps_q[r][:], 1.0 / D, None, ALU.mult)
            v.tensor_tensor(S[:, 2 * TT:3 * TT], S[:, 0:TT], S[:, 0:TT], ALU.mult)
            v.scalar_tensor_tensor(S[:, 3 * TT:4 * TT], S[:, TT:2 * TT], EPS,
                                   S[:, 2 * TT:3 * TT], ALU.add, ALU.subtract).then_inc(s, 1)
        ctx.emit("vector", _st1)
        stats_evt[r] = st1
        ctx.wait("scalar", DV, st1)
        ac2 = AC.inc(1)

        def _sqr(s, r=r, sA=AC.h):
            S = stat[r]
            s.activation(S[:, 4 * TT:5 * TT], S[:, 3 * TT:4 * TT], AF.Sqrt,
                         bias=0.0, scale=1.0).then_inc(sA, 1)
        ctx.emit("scalar", _sqr)
        ctx.wait("vector", AC, ac2)
        st2 = DV.inc(1)

        def _st2(v, r=r, s=DV.h):
            S = stat[r]
            v.reciprocal(S[:, 5 * TT:6 * TT], S[:, 4 * TT:5 * TT])
            v.scalar_tensor_tensor(S[:, 2 * TT:3 * TT], S[:, 0:TT], -1.0,
                                   S[:, 5 * TT:6 * TT], ALU.mult, ALU.mult).then_inc(s, 1)
        ctx.emit("vector", _st2)

        ctx.wait("tensor", DV, st2)
        if apply_evt[r] is not None:
            ctx.wait("tensor", DV, apply_evt[r])
        pe2 = PE.inc(1)

        def _bc(t, r=r, s=PE.h, ctx=ctx):
            S = stat[r]
            t.matmul(ps_a[r][:], ctx.ones1[:], S[:, 5 * TT:6 * TT], start=True, stop=True)
            t.matmul(ps_b[r][:], ctx.ones1[:], S[:, 2 * TT:3 * TT],
                     start=True, stop=True).then_inc(s, 1)
        ctx.emit("tensor", _bc)

        ctx.wait("vector", PE, pe2)
        ctx.wait("vector", LD, ld_tgt)
        if affine:
            ctx.wait("vector", ctx.sem("VLD"), vld_tgt)
            vld_tgt = None
        if store_evt[r] is not None:
            ctx.wait("vector", ST, store_evt[r])
        ap_tgt = DV.inc(1)

        def _ap(v, r=r, s=DV.h, outs=outs, affine=affine, has_res=res is not None):
            last = None
            for c in range(NCH):
                last = v.tensor_tensor(tmp[c][:], xf[r][c][:], ps_a[r][:], ALU.mult)
            prim = outs[0][0]
            for c in range(NCH):
                if affine is None and not has_res:
                    last = v.tensor_tensor(ot[prim][r][c][:], tmp[c][:], ps_b[r][:], ALU.add)
                else:
                    last = v.tensor_tensor(tmp2[c][:], tmp[c][:], ps_b[r][:], ALU.add)
            if affine is not None:
                for c in range(NCH):
                    dst = tmp[c] if has_res else ot[prim][r][c]
                    last = v.tensor_scalar(dst[:], tmp2[c][:], gt[:, c:c + 1],
                                           bt[:, c:c + 1], ALU.mult, ALU.add)
            if has_res:
                srcs = tmp if affine is not None else tmp2
                for c in range(NCH):
                    last = v.tensor_tensor(ot[prim][r][c][:], srcs[c][:], rs[r][c][:], ALU.add)
            for (oname, odt) in outs[1:]:
                for c in range(NCH):
                    last = v.tensor_copy(ot[oname][r][c][:], ot[prim][r][c][:])
            last.then_inc(s, 1)
        ctx.emit("vector", _ap)
        free_evt[r] = ap_tgt
        apply_evt[r] = ap_tgt

        ctx.wait("scalar", DV, ap_tgt)
        st_tgt = ST.inc(16 * NCH * len(outs))

        def _stt(s, r=r, i=i, sS=ST.h, outs=outs):
            for (oname, odt) in outs:
                dst = ctx.dram[oname]
                for c in range(NCH):
                    s.dma_start(out=dst[c * 128:(c + 1) * 128, i * TT:(i + 1) * TT],
                                in_=ot[oname][r][c][:]).then_inc(sS, 16)
        ctx.emit("scalar", _stt)
        store_evt[r] = st_tgt

    ctx.wait("sync", ST, ST.val)
    scope.close()



def emit_l0_attn(ctx, dbg=False):
    emit_consts(ctx)
    scope = ExitStack()
    xn = ctx.dram["xn0"]
    att = ctx.dram["att"]
    LD = ctx.sem("LD"); PE = ctx.sem("PE"); DV = ctx.sem("DV")
    AC = ctx.sem("AC"); ST = ctx.sem("ST"); WL = ctx.sem("WL")
    qwT = ctx.dram["qwT"]; kkwT = ctx.dram["kkwT"]; vwT = ctx.dram["vwT"]
    projT = ctx.dram["projT"]

    qw = [ctx.sbuf(f"a_qw_{c}", [128, D], BF16, scope) for c in range(NCH)]
    kvw = [ctx.sbuf(f"a_kvw_{c}", [128, 128], BF16, scope) for c in range(NCH)]
    vw = [ctx.sbuf(f"a_vw_{c}", [128, 64], BF16, scope) for c in range(NCH)]
    pw = [ctx.sbuf(f"a_pw_{c}", [128, D], BF16, scope) for c in range(NCH)]
    wl_tgt = WL.inc(16 * NCH * 4)

    def _wl(g, s=WL.h):
        for c in range(NCH):
            g.dma_start(out=qw[c][:], in_=qwT[c * 128:(c + 1) * 128, :]).then_inc(s, 16)
            g.dma_start(out=kvw[c][:], in_=kkwT[c * 128:(c + 1) * 128, :]).then_inc(s, 16)
            g.dma_start(out=vw[c][:], in_=vwT[c * 128:(c + 1) * 128, :]).then_inc(s, 16)
            g.dma_start(out=pw[c][:], in_=projT[c * 128:(c + 1) * 128, :]).then_inc(s, 16)
    ctx.emit("gpsimd", _wl)
    pvt, pv_tgt = emit_vec_loads(ctx, ["proj_b"], scope)
    projb = pvt["proj_b"]

    xnb = [[ctx.sbuf(f"a_xn{r}_{c}", [128, TT], BF16, scope) for c in range(NCH)]
           for r in range(2)]
    qsb = [[ctx.sbuf(f"a_q{r}_{c}", [128, TT], BF16, scope) for c in range(NCH)]
           for r in range(2)]
    kvsb = [ctx.sbuf(f"a_kv{r}", [128, TT], BF16, scope) for r in range(2)]
    vsb = [ctx.sbuf(f"a_v{r}", [64, TT], BF16, scope) for r in range(2)]
    vT = [[ctx.sbuf(f"a_vT{r}_{b}", [128, 128], BF16, scope) for b in range(2)]
          for r in range(2)]
    pn_sb = [ctx.sbuf(f"a_pn{p}", [128, 256], BF16, scope) for p in range(2)]
    pn2_sb = [ctx.sbuf(f"a_pn2{p}", [128, 256], BF16, scope) for p in range(2)]
    dbg_sc_sb = ctx.sbuf("a_dbgsc", [128, 256], F32, scope) if dbg else None
    pT_sb = [ctx.sbuf(f"a_pT{p}", [128, 256], BF16, scope) for p in range(2)]
    sm = [ctx.sbuf(f"a_sm{p}", [128, 4], F32, scope) for p in range(2)]
    ao = [[ctx.sbuf(f"a_ao{r}_{c}", [128, TT], BF16, scope) for c in range(NCH)]
          for r in range(2)]
    attb = [[ctx.sbuf(f"a_at{r}_{c}", [128, TT], F32, scope) for c in range(NCH)]
            for r in range(2)]

    pA = [ctx.psum(f"a_pA{j}", [128, TT], F32, scope) for j in range(2)]
    pKV = ctx.psum("a_pKV", [128, TT], F32, scope)
    pSC = [ctx.psum(f"a_pSC{p}", [128, 256], F32, scope) for p in range(2)]
    pTR = ctx.psum("a_pTR", [128, 256], BF16, scope)
    pAV = [ctx.psum(f"a_pAV{p}", [64, 128], F32, scope) for p in range(2)]

    xn_free = [None, None]
    q_free = [None, None]
    ao_free = [None, None]
    attb_free = [None, None]
    kv_ev_r = [None, None]
    pA_free = [None] * 2
    pSC_free = [None] * 2
    pn_free = [None] * 2
    pT_free = [None] * 2
    pTR_free = None
    pAV_free = [None] * 2
    sm_free = [None] * 2
    it = 0

    for i in range(NT):
        r = i % 2
        if xn_free[r] is not None:
            ctx.wait("sync", PE, xn_free[r])
        ld_tgt = LD.inc(16 * NCH)

        def _ld(sy, r=r, i=i, s=LD.h):
            for c in range(NCH):
                sy.dma_start(out=xnb[r][c][:],
                             in_=xn[c * 128:(c + 1) * 128, i * TT:(i + 1) * TT]).then_inc(s, 16)
        ctx.emit("sync", _ld)

        ctx.wait("tensor", LD, ld_tgt)
        if i == 0:
            ctx.wait("tensor", WL, wl_tgt)
            ctx.wait("tensor", ctx.sem("GP"), ctx.ident_ready)
        if kv_ev_r[r] is not None:
            ctx.wait("tensor", AC, kv_ev_r[r])
        if q_free[r] is not None:
            ctx.wait("tensor", PE, q_free[r])
        if pA_free[0] is not None:
            ctx.wait("tensor", AC, pA_free[0])
        pe_kv = PE.inc(1)

        def _kv(t, r=r, s=PE.h):
            for k in range(NCH):
                t.matmul(pKV[:], kvw[k][:], xnb[r][k][:],
                         start=(k == 0), stop=(k == NCH - 1))
            for k in range(NCH):
                m = t.matmul(pA[0][0:64, :], vw[k][:], xnb[r][k][:],
                             start=(k == 0), stop=(k == NCH - 1))
            m.then_inc(s, 1)
        ctx.emit("tensor", _kv)
        ctx.wait("scalar", PE, pe_kv)
        if q_free[r] is not None:
            ctx.wait("scalar", PE, q_free[r])
        kv_ev = AC.inc(1)
        kv_ev_r[r] = kv_ev
        pA_free[0] = kv_ev

        def _kve(sc, r=r, s=AC.h):
            sc.activation(kvsb[r][:], pKV[:], AF.Copy, bias=0.0, scale=1.0)
            sc.activation(vsb[r][:], pA[0][0:64, :], AF.Copy, bias=0.0,
                          scale=1.0).then_inc(s, 1)
        ctx.emit("scalar", _kve)

        ctx.wait("tensor", AC, kv_ev)
        if ao_free[r] is not None:
            ctx.wait("tensor", PE, ao_free[r])
        if pTR_free is not None:
            ctx.wait("tensor", DV, pTR_free)
        pe_vt = PE.inc(1)

        def _vt(t, r=r, s=PE.h):
            for b in range(2):
                m = t.transpose(pTR[:, b * 128:(b + 1) * 128].rearrange("p (x f) -> p x f", x=1).squeeze(1) if False else pTR[:, b * 128:(b + 1) * 128],
                                kvsb[r][64:128, b * 256:(b + 1) * 256].rearrange("p (x f) -> x p f", x=2)[0] if False else kvsb[r][64:128, b * 256 + 0:b * 256 + 128],
                                ctx.identb[0:64, 0:64])
            m.then_inc(s, 1)
        ctx.emit("tensor", _vt)
        ctx.ops["tensor"].pop()
        pe_vt = PE.val

        def _vt2(t, r=r, s=PE.h):
            for b in range(2):
                for tk in range(2):
                    m = t.transpose(pTR[:, (b * 2 + tk) * 64:(b * 2 + tk + 1) * 64],
                                    vsb[r][0:64, b * 256 + tk * 128: b * 256 + (tk + 1) * 128],
                                    ctx.identb[0:64, 0:64])
            m.then_inc(s, 1)
        ctx.emit("tensor", _vt2)
        ctx.wait("vector", PE, pe_vt)
        dv_vt = DV.inc(1)
        pTR_free = dv_vt

        def _vte(v, r=r, s=DV.h):
            v.tensor_copy(vT[r][0][:, 0:128].rearrange("p (a b) -> p a b", a=2) if False else vT[r][0][:], pTR[:, 0:128])
            v.tensor_copy(vT[r][1][:], pTR[:, 128:256]).then_inc(s, 1)
        ctx.emit("vector", _vte)

        q_ev = []
        for oc in range(NCH):
            j = oc % 2
            if pA_free[j] is not None:
                ctx.wait("tensor", AC, pA_free[j])
            pe_q = PE.inc(1)

            def _q(t, r=r, oc=oc, j=j, s=PE.h):
                for k in range(NCH):
                    m = t.matmul(pA[j][:], qw[k][:, oc * 128:(oc + 1) * 128],
                                 xnb[r][k][:], start=(k == 0), stop=(k == NCH - 1))
                m.then_inc(s, 1)
            ctx.emit("tensor", _q)
            ctx.wait("scalar", PE, pe_q)
            if q_free[r] is not None and oc == 0:
                ctx.wait("scalar", PE, q_free[r])
            ac_q = AC.inc(1)
            pA_free[j] = ac_q
            q_ev.append(ac_q)

            def _qe(sc, r=r, oc=oc, j=j, s=AC.h):
                sc.activation(qsb[r][oc][:], pA[j][:], AF.Copy, bias=0.0,
                              scale=0.125).then_inc(s, 1)
            ctx.emit("scalar", _qe)
        q_all_ev = q_ev[-1]

        last_sc = None
        last_av_dv = None
        for bi in range(2):
            for h in range(16):
                for hf in range(2):
                    p = it % 2
                    it += 1
                    ctx.wait("tensor", AC, q_all_ev)
                    if pSC_free[p] is not None:
                        ctx.wait("tensor", AC, pSC_free[p])
                    pe_sc = PE.inc(1)
                    last_sc = pe_sc

                    def _sc(t, r=r, bi=bi, h=h, hf=hf, p=p, s=PE.h):
                        q_sl = qsb[r][h // 2][(h % 2) * 64:(h % 2) * 64 + 64,
                                              bi * 256 + hf * 128: bi * 256 + (hf + 1) * 128]
                        k_sl = kvsb[r][(h % 2) * 64:(h % 2) * 64 + 64,
                                        bi * 256:(bi + 1) * 256]
                        t.matmul(pSC[p][:], q_sl, k_sl, start=True, stop=True).then_inc(s, 1)
                    ctx.emit("tensor", _sc)
                    if dbg and it == 1:
                        ctx.wait("vector", PE, pe_sc)
                        dv_d = ctx.sem("DV").inc(1)

                        def _d0(v, p=p, s=ctx.sems["DV"].h):
                            v.tensor_copy(dbg_sc_sb[:], pSC[p][:]).then_inc(s, 1)
                        ctx.emit("vector", _d0)
                        ctx.wait("gpsimd", ctx.sems["DV"], dv_d)
                        dtg = ST.inc(16)

                        def _d0s(g, s=ST.h):
                            g.dma_start(out=ctx.dram["dbg_sc"], in_=dbg_sc_sb[:]).then_inc(s, 16)
                        ctx.emit("gpsimd", _d0s)
                    ctx.wait("vector", PE, pe_sc)
                    if sm_free[p] is not None:
                        ctx.wait("vector", AC, sm_free[p])
                    dv_nm = DV.inc(1)

                    def _nm(v, p=p, s=DV.h):
                        v.tensor_reduce(sm[p][:, 1:2], pSC[p][:], AX.X, ALU.max,
                                        negate=True).then_inc(s, 1)
                    ctx.emit("vector", _nm)
                    ctx.wait("scalar", DV, dv_nm)
                    ac_exp = AC.inc(1)
                    pSC_free[p] = ac_exp
                    sm_free[p] = ac_exp

                    def _ex(sc, p=p, s=AC.h):
                        sc.activation(pn_sb[p][:], pSC[p][:], AF.Exp, bias=sm[p][:, 1:2],
                                      scale=1.0, accum_out=sm[p][:, 2:3]).then_inc(s, 1)
                    ctx.emit("scalar", _ex)
                    ctx.wait("vector", AC, ac_exp)
                    if pn_free[p] is not None:
                        ctx.wait("vector", PE, pn_free[p])
                    selfw = DV.inc(1)
                    dv_pn = DV.inc(1)

                    def _pn(v, p=p, s=DV.h, selfw=selfw):
                        v.reciprocal(sm[p][:, 3:4], sm[p][:, 2:3]).then_inc(s, 1)
                        v.wait_ge(s, selfw)
                        v.tensor_scalar(pn2_sb[p][:], pn_sb[p][:], sm[p][:, 3:4], None,
                                        ALU.mult).then_inc(s, 1)
                    ctx.emit("vector", _pn)
                    if dbg and it == 1:
                        ctx.wait("gpsimd", DV, dv_pn)
                        ctx.wait("gpsimd", AC, ac_exp)
                        dtg2 = ST.inc(16 * 4)

                        def _d1s(g, p=p, s=ST.h):
                            g.dma_start(out=ctx.dram["dbg_pn"], in_=pn_sb[p][:]).then_inc(s, 16)
                            g.dma_start(out=ctx.dram["dbg_pn2"], in_=pn2_sb[p][:]).then_inc(s, 16)
                            g.dma_start(out=ctx.dram["dbg_sm"], in_=sm[p][:]).then_inc(s, 16)
                            g.dma_start(out=ctx.dram["dbg_ao"][0:64, 0:128] if False else ctx.dram["dbg_sm"], in_=sm[p][:]).then_inc(s, 16)
                        ctx.emit("gpsimd", _d1s)
                    ctx.wait("tensor", DV, dv_pn)
                    ctx.wait("tensor", DV, pTR_free)
                    pe_tr = PE.inc(1)
                    pn_free[p] = pe_tr

                    def _tr(t, p=p, s=PE.h):
                        t.transpose(pTR[:, 0:128], pn2_sb[p][:, 0:128], ctx.identb[:])
                        t.transpose(pTR[:, 128:256], pn2_sb[p][:, 128:256],
                                    ctx.identb[:]).then_inc(s, 1)
                    ctx.emit("tensor", _tr)
                    ctx.wait("vector", PE, pe_tr)
                    if pT_free[p] is not None:
                        ctx.wait("vector", PE, pT_free[p])
                    dv_pt = DV.inc(1)
                    pTR_free = dv_pt

                    def _pte(v, p=p, s=DV.h):
                        v.tensor_copy(pT_sb[p][:], pTR[:]).then_inc(s, 1)
                    ctx.emit("vector", _pte)
                    ctx.wait("tensor", DV, dv_pt)
                    if pAV_free[p] is not None:
                        ctx.wait("tensor", DV, pAV_free[p])
                    pe_av = PE.inc(1)
                    pT_free[p] = pe_av

                    def _av(t, r=r, bi=bi, p=p, s=PE.h):
                        t.matmul(pAV[p][:], vT[r][bi][0:128, 0:64], pT_sb[p][:, 0:128],
                                 start=True, stop=False)
                        t.matmul(pAV[p][:], vT[r][bi][0:128, 64:128], pT_sb[p][:, 128:256],
                                 start=False, stop=True).then_inc(s, 1)
                    ctx.emit("tensor", _av)
                    ctx.wait("vector", PE, pe_av)
                    dv_ao = DV.inc(1)
                    pAV_free[p] = dv_ao
                    last_av_dv = dv_ao

                    def _aoe(v, r=r, bi=bi, h=h, hf=hf, p=p, s=DV.h):
                        v.tensor_copy(ao[r][h // 2][(h % 2) * 64:(h % 2) * 64 + 64,
                                                    bi * 256 + hf * 128: bi * 256 + (hf + 1) * 128],
                                      pAV[p][:]).then_inc(s, 1)
                    ctx.emit("vector", _aoe)
        q_free[r] = last_sc

        for oc in range(NCH):
            j = oc % 2
            ctx.wait("tensor", DV, last_av_dv)
            if pA_free[j] is not None:
                ctx.wait("tensor", AC, pA_free[j])
            pe_p = PE.inc(1)

            def _pj(t, r=r, oc=oc, j=j, s=PE.h):
                for k in range(NCH):
                    m = t.matmul(pA[j][:], pw[k][:, oc * 128:(oc + 1) * 128],
                                 ao[r][k][:], start=(k == 0), stop=(k == NCH - 1))
                m.then_inc(s, 1)
            ctx.emit("tensor", _pj)
            if oc == NCH - 1:
                ao_free[r] = pe_p
            ctx.wait("vector", PE, pe_p)
            if i == 0 and oc == 0:
                ctx.wait("vector", ctx.sem("VLD"), pv_tgt)
            if attb_free[r] is not None and oc == 0:
                ctx.wait("vector", ST, attb_free[r])
            ac_p = AC.inc(1)
            pA_free[j] = ac_p

            def _pje(v, r=r, oc=oc, j=j, sA=AC.h):
                v.tensor_scalar(attb[r][oc][:], pA[j][:], projb[:, oc:oc + 1], None,
                                ALU.add).then_inc(sA, 1)
            ctx.emit("vector", _pje)
        xn_free[r] = q_free[r]

        ctx.wait("gpsimd", AC, AC.val)
        st_tgt = ST.inc(16 * NCH)
        attb_free[r] = st_tgt

        def _st(g, r=r, i=i, s=ST.h):
            for c in range(NCH):
                g.dma_start(out=att[c * 128:(c + 1) * 128, i * TT:(i + 1) * TT],
                            in_=attb[r][c][:]).then_inc(s, 16)
        ctx.emit("gpsimd", _st)
        if dbg:
            ctx.wait("gpsimd", DV, DV.val)
            dst_tgt = ST.inc(16 * (NCH * 2 + 2))

            def _std(g, r=r, i=i, s=ST.h):
                for c in range(NCH):
                    g.dma_start(out=ctx.dram["dbg_q"][c * 128:(c + 1) * 128, i * TT:(i + 1) * TT],
                                in_=qsb[r][c][:]).then_inc(s, 16)
                    g.dma_start(out=ctx.dram["dbg_ao"][c * 128:(c + 1) * 128, i * TT:(i + 1) * TT],
                                in_=ao[r][c][:]).then_inc(s, 16)
                g.dma_start(out=ctx.dram["dbg_kk"][:, i * TT:(i + 1) * TT],
                            in_=kvsb[r][:]).then_inc(s, 16)
                g.dma_start(out=ctx.dram["dbg_v"][:, i * TT:(i + 1) * TT],
                            in_=vsb[r][:]).then_inc(s, 16)
            ctx.emit("gpsimd", _std)

    ctx.wait("sync", ST, ST.val)
    scope.close()


def emit_l0_ffn(ctx):
    emit_consts(ctx)
    scope = ExitStack()
    xn = ctx.dram["xn0"]; x0 = ctx.dram["x0"]; att = ctx.dram["att"]
    x1 = ctx.dram["x1"]
    wtR = ctx.dram["wtR"]; wtI = ctx.dram["wtI"]
    LD = ctx.sem("LD"); PE = ctx.sem("PE"); DV = ctx.sem("DV")
    AC = ctx.sem("AC"); ST = ctx.sem("ST"); WL = ctx.sem("WL")

    wr = [ctx.sbuf(f"f_wr_{k}", [128, 512], BF16, scope) for k in range(4)]
    wi = [ctx.sbuf(f"f_wi_{k}", [128, 512], BF16, scope) for k in range(4)]
    win = [ctx.sbuf(f"f_win_{k}", [128, 512], BF16, scope) for k in range(4)]
    pvt, pv_tgt = emit_vec_loads(ctx, ["l0f_br", "l0f_bi"], scope)
    biasv = ctx.sbuf("f_biasv", [128, 8], F32, scope)
    scope2 = ExitStack()
    g1s = {}
    g2s = {}
    wl0 = WL.inc(16 * 2 * (1 + 16))
    for nm in ("R", "I"):
        g1s[nm] = ctx.sbuf(f"f_g1{nm}", [128, 32], F32, scope2)
        g2s[nm] = ctx.sbuf(f"f_g2{nm}", [128, 16 * 512], F32, scope2)

    def _gl(sy, s=WL.h):
        for nm in ("R", "I"):
            sy.dma_start(out=g1s[nm][:], in_=ctx.dram["g1" + nm]).then_inc(s, 16)
            for g in range(16):
                sy.dma_start(out=g2s[nm][:, g * 512:(g + 1) * 512],
                             in_=ctx.dram["g2" + nm][g]).then_inc(s, 16)
    ctx.emit("sync", _gl)
    wg_sb = [ctx.sbuf(f"f_wg{p}", [32, 512], F32, scope2) for p in range(2)]
    pW = [ctx.psum(f"f_pW{p}", [32, 512], F32, scope2) for p in range(2)]
    ctx.wait("tensor", WL, wl0)
    items = [(nm, g) for nm in ("R", "I") for g in range(16)]
    WB = ctx.sem("WB")
    cp_evt = [None, None]
    st_evt = [None, None]
    prev = None
    wb_tgt = None
    for idx, (nm, g) in enumerate(items):
        p = idx % 2
        dstT = wtR if nm == "R" else wtI
        if cp_evt[p] is not None:
            ctx.wait("tensor", DV, cp_evt[p])
        pe_w = PE.inc(1)

        def _wmm(t, nm=nm, g=g, p=p, s=PE.h):
            t.matmul(pW[p][:], g1s[nm][:], g2s[nm][:, g * 512:(g + 1) * 512],
                     start=True, stop=True).then_inc(s, 1)
        ctx.emit("tensor", _wmm)
        ctx.wait("vector", PE, pe_w)
        if st_evt[p] is not None:
            ctx.wait("vector", WB, st_evt[p])
        dv_w = DV.inc(1)
        cp_evt[p] = dv_w

        def _wev(v, p=p, s=DV.h):
            v.tensor_copy(wg_sb[p][:], pW[p][:]).then_inc(s, 1)
        ctx.emit("vector", _wev)
        if prev is not None:
            pdst, pg, pp = prev
            ctx.wait("scalar", DV, dv_w)
            wb_tgt = WB.inc(16)
            st_evt[pp] = wb_tgt

            def _wst(sc, dstT=pdst, g=pg, p=pp, s=WB.h):
                sc.dma_start(out=dstT.rearrange("(d g) f -> d g f", g=16)[:, g, :],
                             in_=wg_sb[p][:]).then_inc(s, 16)
            ctx.emit("scalar", _wst)
        prev = (dstT, g, p)
    fence_tgt = DV.inc(1)

    def _wfence(v, prev=prev, s=DV.h):
        v.tensor_copy(wg_sb[1 - prev[2]][:, 0:4], wg_sb[1 - prev[2]][:, 0:4]).then_inc(s, 1)
    ctx.emit("vector", _wfence)
    pdst, pg, pp = prev
    ctx.wait("scalar", DV, fence_tgt)
    wb_tgt = WB.inc(16)

    def _wstf(sc, dstT=pdst, g=pg, p=pp, s=WB.h):
        sc.dma_start(out=dstT.rearrange("(d g) f -> d g f", g=16)[:, g, :],
                     in_=wg_sb[p][:]).then_inc(s, 16)
    ctx.emit("scalar", _wstf)
    scope2.close()
    ctx.wait("gpsimd", ctx.sem("WB"), wb_tgt)
    wl1 = WL.inc(16 * 8)

    def _wld(g, s=WL.h):
        for k in range(4):
            g.dma_start(out=wr[k][:], in_=wtR[k * 128:(k + 1) * 128, :]).then_inc(s, 16)
            g.dma_start(out=wi[k][:], in_=wtI[k * 128:(k + 1) * 128, :]).then_inc(s, 16)
    ctx.emit("gpsimd", _wld)
    ctx.wait("vector", WL, wl1)
    dv_wn = DV.inc(1)

    def _wneg(v, s=DV.h):
        for k in range(4):
            m = v.tensor_scalar(win[k][:], wi[k][:], -1.0, None, ALU.mult)
        m.then_inc(s, 1)
    ctx.emit("vector", _wneg)
    ctx.wait("vector", ctx.sem("VLD"), pv_tgt)
    dv_bv = DV.inc(1)

    def _bv(v, s=DV.h):
        v.tensor_copy(biasv[:, 0:4], pvt["l0f_br"][:, 0:4])
        v.tensor_copy(biasv[:, 4:8], pvt["l0f_bi"][:, 0:4]).then_inc(s, 1)
    ctx.emit("vector", _bv)

    xnb = [[ctx.sbuf(f"f_xn{r}_{c}", [128, TT], BF16, scope) for c in range(NCH)]
           for r in range(2)]
    xf = [[ctx.sbuf(f"f_x{r}_{c}", [128, TT], F32, scope) for c in range(NCH)]
          for r in range(2)]
    atf = [[ctx.sbuf(f"f_at{r}_{c}", [128, TT], F32, scope) for c in range(NCH)]
           for r in range(2)]
    tmp = [ctx.sbuf(f"f_tmp_{c}", [128, TT], F32, scope) for c in range(NCH)]
    ob = [[ctx.sbuf(f"f_ob{r}_{c}", [128, TT], F32, scope) for c in range(NCH)]
          for r in range(2)]
    pO = [ctx.psum(f"f_pO{j}", [128, TT], F32, scope) for j in range(2)]

    buf_free = [None, None]
    pO_free = [None, None]
    ob_free = [None, None]
    for i in range(NT):
        r = i % 2
        if i == 0:
            ctx.wait("sync", WL, wl1)
            ctx.wait("gpsimd", WL, wl1)
        if buf_free[r] is not None:
            ctx.wait("sync", DV, buf_free[r])
            ctx.wait("gpsimd", DV, buf_free[r])
        ld_tgt = LD.inc(16 * NCH * 2)

        def _ld(sy, r=r, i=i, s=LD.h):
            for c in range(NCH):
                sy.dma_start(out=xnb[r][c][:],
                             in_=xn[c * 128:(c + 1) * 128, i * TT:(i + 1) * TT]).then_inc(s, 16)
            for c in range(NCH):
                sy.dma_start(out=xf[r][c][:],
                             in_=x0[c * 128:(c + 1) * 128, i * TT:(i + 1) * TT]).then_inc(s, 16)
        ctx.emit("sync", _ld)
        GL = ctx.sem("GL")
        gl_tgt = GL.inc(16 * NCH)

        def _lda(g, r=r, i=i, s=GL.h):
            for c in range(NCH):
                g.dma_start(out=atf[r][c][:],
                            in_=att[c * 128:(c + 1) * 128, i * TT:(i + 1) * TT]).then_inc(s, 16)
        ctx.emit("gpsimd", _lda)

        for oc in range(NCH):
            j = oc % 2
            ctx.wait("tensor", LD, ld_tgt)
            if i == 0 and oc == 0:
                ctx.wait("tensor", WL, wl1)
                ctx.wait("tensor", DV, dv_wn)
            if pO_free[j] is not None:
                ctx.wait("tensor", DV, pO_free[j])
            pe_o = PE.inc(1)

            def _om(t, r=r, oc=oc, j=j, s=PE.h):
                if oc < 4:
                    for k in range(4):
                        t.matmul(pO[j][:], wr[k][:, oc * 128:(oc + 1) * 128],
                                 xnb[r][k][:], start=(k == 0), stop=False)
                    for k in range(4):
                        m = t.matmul(pO[j][:], win[k][:, oc * 128:(oc + 1) * 128],
                                     xnb[r][4 + k][:], start=False, stop=(k == 3))
                else:
                    o2 = oc - 4
                    for k in range(4):
                        t.matmul(pO[j][:], wi[k][:, o2 * 128:(o2 + 1) * 128],
                                 xnb[r][k][:], start=(k == 0), stop=False)
                    for k in range(4):
                        m = t.matmul(pO[j][:], wr[k][:, o2 * 128:(o2 + 1) * 128],
                                     xnb[r][4 + k][:], start=False, stop=(k == 3))
                m.then_inc(s, 1)
            ctx.emit("tensor", _om)
            ctx.wait("vector", PE, pe_o)
            if i == 0 and oc == 0:
                ctx.wait("vector", DV, dv_bv)
            ctx.wait("vector", GL, gl_tgt)
            if ob_free[r] is not None and oc == 0:
                ctx.wait("vector", ST, ob_free[r])
            dv_o = DV.inc(1)
            pO_free[j] = dv_o

            def _oe(v, r=r, oc=oc, j=j, s=DV.h):
                v.tensor_scalar(tmp[oc][:], pO[j][:], biasv[:, oc:oc + 1], None, ALU.add)
                v.tensor_tensor(tmp[oc][:], tmp[oc][:], atf[r][oc][:], ALU.add)
                v.tensor_tensor(ob[r][oc][:], tmp[oc][:], xf[r][oc][:],
                                ALU.add).then_inc(s, 1)
            ctx.emit("vector", _oe)
        buf_free[r] = DV.val

        ctx.wait("scalar", DV, DV.val)
        st_tgt = ST.inc(16 * NCH)
        ob_free[r] = st_tgt

        def _st(sc, r=r, i=i, s=ST.h):
            for c in range(NCH):
                sc.dma_start(out=x1[c * 128:(c + 1) * 128, i * TT:(i + 1) * TT],
                             in_=ob[r][c][:]).then_inc(s, 16)
        ctx.emit("scalar", _st)

    ctx.wait("sync", ST, ST.val)
    scope.close()



def emit_matmul(ctx, src_name, w_name, K, outF, outs, bias_name=None,
                lrelu=False, adds=()):
    emit_consts(ctx)
    scope = ExitStack()
    src = ctx.dram[src_name]
    KC = K // 128
    OC = outF // 128
    LD = ctx.sem("LD"); GL = ctx.sem("GL"); PE = ctx.sem("PE")
    DV = ctx.sem("DV"); AC = ctx.sem("AC"); ST = ctx.sem("ST"); WL = ctx.sem("WL")

    wt = [ctx.sbuf(f"mm_w_{k}", [128, outF], BF16, scope) for k in range(KC)]
    wl_tgt = WL.inc(16 * KC)

    def _wl(g, s=WL.h, wsrc=ctx.dram[w_name]):
        for k in range(KC):
            g.dma_start(out=wt[k][:], in_=wsrc[k * 128:(k + 1) * 128, :]).then_inc(s, 16)
    ctx.emit("gpsimd", _wl)

    bias_t = None
    pv_tgt = None
    if bias_name:
        pvt, pv_tgt = emit_vec_loads(ctx, [bias_name], scope)
        bias_t = pvt[bias_name]

    nring = 1 if KC > 8 else 2
    sb = [[ctx.sbuf(f"mm_s{r}_{k}", [128, TT], BF16, scope) for k in range(KC)]
          for r in range(nring)]
    ad = {a: [[ctx.sbuf(f"mm_a_{a}{r}_{c}", [128, TT], F32, scope) for c in range(OC)]
              for r in range(2)] for a in adds}
    tmp = [ctx.sbuf(f"mm_t_{c}", [128, TT], F32, scope) for c in range(2)]
    ot = {nm: [[ctx.sbuf(f"mm_o_{nm}{r}_{c}", [128, TT], dt_, scope) for c in range(OC)]
               for r in range(2)] for nm, dt_ in outs}
    pO = [ctx.psum(f"mm_p{j}", [128, TT], F32, scope) for j in range(2)]

    src_free = [None, None]
    pO_free = [None, None]
    ot_free = [None, None]
    prim = outs[0][0]
    for i in range(NT):
        r = i % nring
        ra = i % 2
        if nring == 1:
            ctx.wait("sync", PE, PE.val)
            ctx.wait("gpsimd", PE, PE.val)
        elif src_free[r] is not None:
            ctx.wait("sync", DV, src_free[r])
            ctx.wait("gpsimd", DV, src_free[r])
        ld_tgt = LD.inc(16 * KC)

        def _ld(sy, r=r, i=i, s=LD.h):
            for k in range(KC):
                sy.dma_start(out=sb[r][k][:],
                             in_=src[k * 128:(k + 1) * 128, i * TT:(i + 1) * TT]).then_inc(s, 16)
        ctx.emit("sync", _ld)
        gl_tgt = None
        if adds:
            gl_tgt = GL.inc(16 * OC * len(adds))

            def _lda(g, r=ra, i=i, s=GL.h):
                for a in adds:
                    asrc = ctx.dram[a]
                    for c in range(OC):
                        g.dma_start(out=ad[a][r][c][:],
                                    in_=asrc[c * 128:(c + 1) * 128,
                                             i * TT:(i + 1) * TT]).then_inc(s, 16)
            ctx.emit("gpsimd", _lda)

        for oc in range(OC):
            j = oc % 2
            if oc == 0:
                ctx.wait("tensor", LD, ld_tgt)
                if i == 0:
                    ctx.wait("tensor", WL, wl_tgt)
            if pO_free[j] is not None:
                ctx.wait("tensor", DV, pO_free[j])
            pe_o = PE.inc(1)

            def _mm(t, r=r, oc=oc, j=j, s=PE.h):
                for k in range(KC):
                    m = t.matmul(pO[j][:], wt[k][:, oc * 128:(oc + 1) * 128],
                                 sb[r][k][:], start=(k == 0), stop=(k == KC - 1))
                m.then_inc(s, 1)
            ctx.emit("tensor", _mm)

            if lrelu:
                ctx.wait("scalar", PE, pe_o)
                ac_e = AC.inc(1)

                def _cp(sc, oc=oc, j=j, s=AC.h, tc=oc % len(tmp)):
                    sc.activation(tmp[tc][:], pO[j][:], AF.Copy, bias=0.0,
                                  scale=1.0).then_inc(s, 1)
                ctx.emit("scalar", _cp)
                ctx.wait("vector", AC, ac_e)
                if ot_free[ra] is not None and oc == 0:
                    ctx.wait("vector", ST, ot_free[ra])
                dv_e = DV.inc(1)
                pO_free[j] = dv_e

                def _lr(v, r=ra, oc=oc, s=DV.h, tc=oc % len(tmp)):
                    v.scalar_tensor_tensor(ot[prim][r][oc][:], tmp[tc][:], 0.01,
                                           tmp[tc][:], ALU.mult, ALU.max).then_inc(s, 1)
                ctx.emit("vector", _lr)
            else:
                ctx.wait("vector", PE, pe_o)
                if i == 0 and oc == 0 and pv_tgt is not None:
                    ctx.wait("vector", ctx.sem("VLD"), pv_tgt)
                if adds and oc == 0:
                    ctx.wait("vector", GL, gl_tgt)
                if ot_free[ra] is not None and oc == 0:
                    ctx.wait("vector", ST, ot_free[ra])
                dv_e = DV.inc(1)
                pO_free[j] = dv_e

                def _ep(v, r=ra, oc=oc, j=j, s=DV.h, tc=oc % len(tmp)):
                    cur = pO[j][:]
                    steps = []
                    if bias_t is not None:
                        steps.append(("bias",))
                    for a in adds:
                        steps.append(("add", a))
                    if not steps:
                        last = v.tensor_copy(ot[prim][r][oc][:], cur)
                    else:
                        for si, stp in enumerate(steps):
                            dst = ot[prim][r][oc][:] if si == len(steps) - 1 else tmp[tc][:]
                            if stp[0] == "bias":
                                last = v.tensor_scalar(dst, cur, bias_t[:, oc:oc + 1],
                                                       None, ALU.add)
                            else:
                                last = v.tensor_tensor(dst, cur, ad[stp[1]][r][oc][:],
                                                       ALU.add)
                            cur = dst
                    for (nm, dt_) in outs[1:]:
                        last = v.tensor_copy(ot[nm][r][oc][:], ot[prim][r][oc][:])
                    last.then_inc(s, 1)
                ctx.emit("vector", _ep)
        src_free[r] = DV.val

        ctx.wait("scalar", DV, DV.val)
        st_tgt = ST.inc(16 * OC * len(outs))
        ot_free[ra] = st_tgt

        def _st(sc, r=ra, i=i, s=ST.h):
            for (nm, dt_) in outs:
                dst = ctx.dram[nm]
                for c in range(OC):
                    sc.dma_start(out=dst[c * 128:(c + 1) * 128, i * TT:(i + 1) * TT],
                                 in_=ot[nm][r][c][:]).then_inc(s, 16)
        ctx.emit("scalar", _st)

    ctx.wait("sync", ST, ST.val)
    scope.close()


def emit_shared(ctx):
    emit_consts(ctx)
    scope = ExitStack()
    xn = ctx.dram["xnL1b"]
    LD = ctx.sem("LD"); GL = ctx.sem("GL"); PE = ctx.sem("PE")
    DV = ctx.sem("DV"); AC = ctx.sem("AC"); ST = ctx.sem("ST"); WL = ctx.sem("WL")

    wg = [ctx.sbuf(f"sh_wg_{k}", [128, D], BF16, scope) for k in range(NCH)]
    wu = [ctx.sbuf(f"sh_wu_{k}", [128, D], BF16, scope) for k in range(NCH)]
    wd = [ctx.sbuf(f"sh_wd_{k}", [128, D], BF16, scope) for k in range(NCH)]
    wl_tgt = WL.inc(16 * NCH * 3)

    def _wl(g, s=WL.h):
        for k in range(NCH):
            g.dma_start(out=wg[k][:], in_=ctx.dram["shgT"][k * 128:(k + 1) * 128, :]).then_inc(s, 16)
            g.dma_start(out=wu[k][:], in_=ctx.dram["shuT"][k * 128:(k + 1) * 128, :]).then_inc(s, 16)
            g.dma_start(out=wd[k][:], in_=ctx.dram["shdT"][k * 128:(k + 1) * 128, :]).then_inc(s, 16)
    ctx.emit("gpsimd", _wl)

    sb = [[ctx.sbuf(f"sh_s{r}_{k}", [128, TT], BF16, scope) for k in range(NCH)]
          for r in range(2)]
    sg = [ctx.sbuf(f"sh_sg_{c}", [128, TT], BF16, scope) for c in range(NCH)]
    pr = [ctx.sbuf(f"sh_pr_{c}", [128, TT], BF16, scope) for c in range(NCH)]
    a1 = [[ctx.sbuf(f"sh_a1{r}_{c}", [128, TT], F32, scope) for c in range(NCH)]
          for r in range(2)]
    a2 = [[ctx.sbuf(f"sh_a2{r}_{c}", [128, TT], F32, scope) for c in range(NCH)]
          for r in range(2)]
    ob = [[ctx.sbuf(f"sh_o{r}_{c}", [128, TT], F32, scope) for c in range(NCH)]
          for r in range(2)]
    tmp = [ctx.sbuf(f"sh_t_{c}", [128, TT], F32, scope) for c in range(2)]
    pO = [ctx.psum(f"sh_p{j}", [128, TT], F32, scope) for j in range(2)]

    src_free = [None, None]
    pO_freeA = [None, None]
    pO_freeD = [None, None]
    ob_free = [None, None]
    sg_free = None
    pr_free = None
    for i in range(NT):
        r = i % 2
        if src_free[r] is not None:
            ctx.wait("sync", PE, src_free[r])
            ctx.wait("gpsimd", PE, src_free[r])
        ld_tgt = LD.inc(16 * NCH)

        def _ld(sy, r=r, i=i, s=LD.h):
            for k in range(NCH):
                sy.dma_start(out=sb[r][k][:],
                             in_=xn[k * 128:(k + 1) * 128, i * TT:(i + 1) * TT]).then_inc(s, 16)
        ctx.emit("sync", _ld)
        gl_tgt = GL.inc(16 * NCH * 2)

        def _lda(g, r=r, i=i, s=GL.h):
            for c in range(NCH):
                g.dma_start(out=a1[r][c][:],
                            in_=ctx.dram["x1"][c * 128:(c + 1) * 128,
                                               i * TT:(i + 1) * TT]).then_inc(s, 16)
            for c in range(NCH):
                g.dma_start(out=a2[r][c][:],
                            in_=ctx.dram["a_out"][c * 128:(c + 1) * 128,
                                                  i * TT:(i + 1) * TT]).then_inc(s, 16)
        ctx.emit("gpsimd", _lda)

        for oc in range(NCH):
            j = oc % 2
            if oc == 0:
                ctx.wait("tensor", LD, ld_tgt)
                if i == 0:
                    ctx.wait("tensor", WL, wl_tgt)
            if pO_freeA[j] is not None:
                ctx.wait("tensor", AC, pO_freeA[j])
            if pO_freeD[j] is not None:
                ctx.wait("tensor", DV, pO_freeD[j])
            pe_1 = PE.inc(1)

            def _m1(t, r=r, oc=oc, j=j, s=PE.h):
                for k in range(NCH):
                    m = t.matmul(pO[j][:], wg[k][:, oc * 128:(oc + 1) * 128],
                                 sb[r][k][:], start=(k == 0), stop=(k == NCH - 1))
                m.then_inc(s, 1)
            ctx.emit("tensor", _m1)
            ctx.wait("scalar", PE, pe_1)
            if sg_free is not None and oc == 0:
                ctx.wait("scalar", PE, sg_free)
            ac_1 = AC.inc(1)
            pO_freeA[j] = ac_1

            def _e1(sc, oc=oc, j=j, s=AC.h):
                sc.activation(sg[oc][:], pO[j][:], AF.Silu, bias=0.0,
                              scale=1.0).then_inc(s, 1)
            ctx.emit("scalar", _e1)
        for oc in range(NCH):
            j = oc % 2
            if pO_freeA[j] is not None:
                ctx.wait("tensor", AC, pO_freeA[j])
            if pO_freeD[j] is not None:
                ctx.wait("tensor", DV, pO_freeD[j])
            pe_2 = PE.inc(1)

            def _m2(t, r=r, oc=oc, j=j, s=PE.h):
                for k in range(NCH):
                    m = t.matmul(pO[j][:], wu[k][:, oc * 128:(oc + 1) * 128],
                                 sb[r][k][:], start=(k == 0), stop=(k == NCH - 1))
                m.then_inc(s, 1)
            ctx.emit("tensor", _m2)
            ctx.wait("vector", PE, pe_2)
            ctx.wait("vector", AC, AC.val)
            if pr_free is not None and oc == 0:
                ctx.wait("vector", PE, pr_free)
            dv_2 = DV.inc(1)
            pO_freeD[j] = dv_2

            def _e2(v, oc=oc, j=j, s=DV.h):
                v.tensor_tensor(pr[oc][:], sg[oc][:], pO[j][:], ALU.mult).then_inc(s, 1)
            ctx.emit("vector", _e2)
        sg_free = PE.val
        for oc in range(NCH):
            j = oc % 2
            ctx.wait("tensor", DV, DV.val)
            if pO_freeD[j] is not None:
                ctx.wait("tensor", DV, pO_freeD[j])
            pe_3 = PE.inc(1)

            def _m3(t, oc=oc, j=j, s=PE.h):
                for k in range(NCH):
                    m = t.matmul(pO[j][:], wd[k][:, oc * 128:(oc + 1) * 128],
                                 pr[k][:], start=(k == 0), stop=(k == NCH - 1))
                m.then_inc(s, 1)
            ctx.emit("tensor", _m3)
            ctx.wait("vector", PE, pe_3)
            ctx.wait("vector", GL, gl_tgt)
            if ob_free[r] is not None and oc == 0:
                ctx.wait("vector", ST, ob_free[r])
            dv_3 = DV.inc(1)
            pO_freeD[j] = dv_3

            def _e3(v, r=r, oc=oc, j=j, s=DV.h, tc=oc % 2):
                v.tensor_tensor(tmp[tc][:], pO[j][:], a1[r][oc][:], ALU.add)
                v.tensor_tensor(ob[r][oc][:], tmp[tc][:], a2[r][oc][:],
                                ALU.add).then_inc(s, 1)
            ctx.emit("vector", _e3)
        pr_free = PE.val
        src_free[r] = PE.val

        ctx.wait("scalar", DV, DV.val)
        st_tgt = ST.inc(16 * NCH)
        ob_free[r] = st_tgt

        def _st(sc, r=r, i=i, s=ST.h):
            for c in range(NCH):
                sc.dma_start(out=ctx.dram["base"][c * 128:(c + 1) * 128,
                                                 i * TT:(i + 1) * TT],
                             in_=ob[r][c][:]).then_inc(s, 16)
        ctx.emit("scalar", _st)

    ctx.wait("sync", ST, ST.val)
    scope.close()



def emit_mamba_a(ctx):
    emit_consts(ctx)
    scope = ExitStack()
    s1 = ctx.dram["s1"]
    LD = ctx.sem("LD"); PE = ctx.sem("PE"); DV = ctx.sem("DV")
    AC = ctx.sem("AC"); ST = ctx.sem("ST"); WL = ctx.sem("WL")

    inp_w = [ctx.sbuf(f"ma_iw_{k}", [128, 2 * D], BF16, scope) for k in range(NCH)]
    xp_w = [ctx.sbuf(f"ma_xp_{k}", [128, 128], BF16, scope) for k in range(NCH)]
    dt_w = ctx.sbuf("ma_dtw", [64, D], BF16, scope)
    wl_tgt = WL.inc(16 * (NCH * 2 + 1))

    def _wl(g, s=WL.h):
        for k in range(NCH):
            g.dma_start(out=inp_w[k][:],
                        in_=ctx.dram["inpT"][k * 128:(k + 1) * 128, :]).then_inc(s, 16)
            g.dma_start(out=xp_w[k][:],
                        in_=ctx.dram["xprojT"][k * 128:(k + 1) * 128, :]).then_inc(s, 16)
        g.dma_start(out=dt_w[:], in_=ctx.dram["dtprojT"]).then_inc(s, 16)
    ctx.emit("gpsimd", _wl)
    pvt, pv_tgt = emit_vec_loads(ctx, ["convw0", "convw1", "convb", "dtprojb"], scope)
    c0v, c1v, cbv, dbv = (pvt[n] for n in ("convw0", "convw1", "convb", "dtprojb"))

    sb = [[ctx.sbuf(f"ma_s{r}_{k}", [128, TT], BF16, scope) for k in range(NCH)]
          for r in range(2)]
    xin = [[ctx.sbuf(f"ma_xi{r}_{c}", [128, TT], F32, scope) for c in range(NCH)]
           for r in range(2)]
    szt = [ctx.sbuf(f"ma_sz_{c}", [128, TT], F32, scope) for c in range(NCH)]
    pre = [ctx.sbuf(f"ma_pre_{c}", [128, TT], F32, scope) for c in range(NCH)]
    xct = [ctx.sbuf(f"ma_xc_{c}", [128, TT], F32, scope) for c in range(NCH)]
    xcbt = [ctx.sbuf(f"ma_xcb_{c}", [128, TT], BF16, scope) for c in range(NCH)]
    dtb = ctx.sbuf("ma_dtb", [64, TT], BF16, scope)
    bc_sb = ctx.sbuf("ma_bc", [128, TT], F32, scope)
    est = [ctx.sbuf(f"ma_e_{j}", [128, TT], F32, scope) for j in range(2)]
    dl = [ctx.sbuf(f"ma_dl_{c}", [128, TT], F32, scope) for c in range(NCH)]
    bcol = ctx.sbuf("ma_bcol", [128, NCH], F32, scope)
    pO = [ctx.psum(f"ma_p{j}", [128, TT], F32, scope) for j in range(2)]
    pD = ctx.psum("ma_pD", [128, TT], F32, scope)
    pT = [ctx.psum(f"ma_pT{j}", [128, TT], F32, scope) for j in range(2)]

    dv0 = DV.inc(1)

    def _z(v, s=DV.h):
        v.memset(bcol[:], 0.0).then_inc(s, 1)
    ctx.emit("vector", _z)

    sb_free = [None, None]
    pO_free = [None, None]
    pT_free = [None, None]
    st_done = None
    conv_done = None
    xdbl_done = None
    for i in range(NT):
        r = i % 2
        if sb_free[r] is not None:
            ctx.wait("sync", PE, sb_free[r])
        ld_tgt = LD.inc(16 * NCH)

        def _ld(sy, r=r, i=i, s=LD.h):
            for k in range(NCH):
                sy.dma_start(out=sb[r][k][:],
                             in_=s1[k * 128:(k + 1) * 128, i * TT:(i + 1) * TT]).then_inc(s, 16)
        ctx.emit("sync", _ld)

        for oc in range(16):
            j = oc % 2
            if oc == 0:
                ctx.wait("tensor", LD, ld_tgt)
                if i == 0:
                    ctx.wait("tensor", WL, wl_tgt)
            if pO_free[j] is not None:
                ctx.wait("tensor", pO_free[j][0], pO_free[j][1])
            pe_o = PE.inc(1)

            def _mm(t, r=r, oc=oc, j=j, s=PE.h):
                for k in range(NCH):
                    m = t.matmul(pO[j][:], inp_w[k][:, oc * 128:(oc + 1) * 128],
                                 sb[r][k][:], start=(k == 0), stop=(k == NCH - 1))
                m.then_inc(s, 1)
            ctx.emit("tensor", _mm)
            if oc < 8:
                ctx.wait("vector", PE, pe_o)
                if conv_done is not None and oc == 0:
                    ctx.wait("vector", DV, conv_done)
                dv_e = DV.inc(1)
                pO_free[j] = (DV, dv_e)

                def _xe(v, r=r, oc=oc, j=j, s=DV.h):
                    v.tensor_copy(xin[r][oc][:], pO[j][:]).then_inc(s, 1)
                ctx.emit("vector", _xe)
            else:
                ctx.wait("scalar", PE, pe_o)
                if st_done is not None and oc == 8:
                    ctx.wait("scalar", ST, st_done)
                ac_e = AC.inc(1)
                pO_free[j] = (AC, ac_e)

                def _ze(sc, oc=oc, j=j, s=AC.h):
                    sc.activation(szt[oc - 8][:], pO[j][:], AF.Silu, bias=0.0,
                                  scale=1.0).then_inc(s, 1)
                ctx.emit("scalar", _ze)
        xin_ev = DV.val
        z_ev = AC.val

        ctx.wait("vector", DV, xin_ev)
        if i == 0:
            ctx.wait("vector", ctx.sem("VLD"), pv_tgt)
        if st_done is not None:
            ctx.wait("vector", ST, st_done)
        dv_cv = DV.inc(1)

        def _cv(v, r=r, s=DV.h):
            for c in range(NCH):
                v.tensor_scalar(pre[c][:], xin[r][c][:], c1v[:, c:c + 1],
                                cbv[:, c:c + 1], ALU.mult, ALU.add)
            for c in range(NCH):
                v.scalar_tensor_tensor(pre[c][:, 1:TT], xin[r][c][:, 0:TT - 1],
                                       c0v[:, c:c + 1], pre[c][:, 1:TT],
                                       ALU.mult, ALU.add)
            for c in range(NCH):
                v.scalar_tensor_tensor(pre[c][:, 0:1], bcol[:, c:c + 1],
                                       c0v[:, c:c + 1], pre[c][:, 0:1],
                                       ALU.mult, ALU.add)
            for c in range(NCH):
                m = v.tensor_copy(bcol[:, c:c + 1], xin[r][c][:, TT - 1:TT])
            m.then_inc(s, 1)
        ctx.emit("vector", _cv)
        conv_done = dv_cv
        ctx.wait("scalar", DV, dv_cv)
        ac_xc = AC.inc(1)

        def _xc(sc, s=AC.h):
            for c in range(NCH):
                m = sc.activation(xct[c][:], pre[c][:], AF.Silu, bias=0.0, scale=1.0)
            m.then_inc(s, 1)
        ctx.emit("scalar", _xc)
        ctx.wait("vector", AC, ac_xc)
        dv_xb = DV.inc(1)

        def _xb(v, s=DV.h):
            for c in range(NCH):
                m = v.tensor_copy(xcbt[c][:], xct[c][:])
            m.then_inc(s, 1)
        ctx.emit("vector", _xb)

        ctx.wait("tensor", DV, dv_xb)
        if xdbl_done is not None:
            ctx.wait("tensor", xdbl_done[0], xdbl_done[1])
        pe_d = PE.inc(1)

        def _xd(t, s=PE.h):
            for k in range(NCH):
                m = t.matmul(pD[:], xp_w[k][:], xcbt[k][:],
                             start=(k == 0), stop=(k == NCH - 1))
            m.then_inc(s, 1)
        ctx.emit("tensor", _xd)
        ctx.wait("scalar", PE, pe_d)
        ac_d = AC.inc(1)

        def _de(sc, s=AC.h):
            sc.activation(dtb[:], pD[0:64, :], AF.Copy, bias=0.0, scale=1.0).then_inc(s, 1)
        ctx.emit("scalar", _de)
        ctx.wait("vector", PE, pe_d)
        dv_bc = DV.inc(1)
        xdbl_done = (DV, dv_bc)

        def _bce(v, s=DV.h):
            v.tensor_copy(bc_sb[64:80, :], pD[64:80, :])
            v.tensor_copy(bc_sb[96:112, :], pD[96:112, :]).then_inc(s, 1)
        ctx.emit("vector", _bce)

        for oc in range(NCH):
            j = oc % 2
            if oc == 0:
                ctx.wait("tensor", AC, ac_d)
            if pT_free[j] is not None:
                ctx.wait("tensor", AC, pT_free[j])
            pe_t = PE.inc(1)

            def _dm(t, oc=oc, j=j, s=PE.h):
                t.matmul(pT[j][:], dt_w[:, oc * 128:(oc + 1) * 128], dtb[:],
                         start=True, stop=True).then_inc(s, 1)
            ctx.emit("tensor", _dm)
            ctx.wait("scalar", PE, pe_t)
            ac_t = AC.inc(1)
            pT_free[j] = ac_t

            def _sp(sc, oc=oc, j=j, s=AC.h, ec=oc % 2):
                sc.activation(est[ec][:], pT[j][:], AF.Exp, bias=dbv[:, oc:oc + 1],
                              scale=1.0)
                sc.activation(dl[oc][:], est[ec][:], AF.Ln, bias=1.0,
                              scale=1.0).then_inc(s, 1)
            ctx.emit("scalar", _sp)
        sp_ev = AC.val
        sb_free[r] = PE.val

        ctx.wait("gpsimd", AC, sp_ev)
        ctx.wait("gpsimd", DV, DV.val)
        st_tgt = ST.inc(16 * (NCH * 4 + 2))
        st_done = st_tgt

        def _st(g, i=i, s=ST.h):
            for c in range(NCH):
                sl = (slice(c * 128, (c + 1) * 128), slice(i * TT, (i + 1) * TT))
                g.dma_start(out=ctx.dram["xc32"][sl[0], sl[1]], in_=xct[c][:]).then_inc(s, 16)
                g.dma_start(out=ctx.dram["xcb"][sl[0], sl[1]], in_=xcbt[c][:]).then_inc(s, 16)
                g.dma_start(out=ctx.dram["sz"][sl[0], sl[1]], in_=szt[c][:]).then_inc(s, 16)
                g.dma_start(out=ctx.dram["delta"][sl[0], sl[1]], in_=dl[c][:]).then_inc(s, 16)
            g.dma_start(out=ctx.dram["Bm"][:, i * TT:(i + 1) * TT],
                        in_=bc_sb[64:80, :]).then_inc(s, 16)
            g.dma_start(out=ctx.dram["Cm"][:, i * TT:(i + 1) * TT],
                        in_=bc_sb[96:112, :]).then_inc(s, 16)
        ctx.emit("gpsimd", _st)

    ctx.wait("sync", ST, ST.val)
    scope.close()


def emit_mamba_b(ctx):
    emit_consts(ctx)
    scope = ExitStack()
    LD = ctx.sem("LD"); PE = ctx.sem("PE"); DV = ctx.sem("DV")
    AC = ctx.sem("AC"); ST = ctx.sem("ST"); WL = ctx.sem("WL")
    PL = ctx.sem("PL")

    opw = [ctx.sbuf(f"mb_ow_{k}", [128, D], BF16, scope) for k in range(NCH)]
    alog = [ctx.sbuf(f"mb_al_{c}", [128, 16], F32, scope) for c in range(NCH)]
    negA = [ctx.sbuf(f"mb_na_{c}", [128, 16], F32, scope) for c in range(NCH)]
    hst = [ctx.sbuf(f"mb_h_{c}", [128, 16], F32, scope) for c in range(NCH)]
    wl_tgt = WL.inc(16 * (NCH * 2))

    def _wl(g, s=WL.h):
        for k in range(NCH):
            g.dma_start(out=opw[k][:],
                        in_=ctx.dram["outprojT"][k * 128:(k + 1) * 128, :]).then_inc(s, 16)
            g.dma_start(out=alog[k][:],
                        in_=ctx.dram["Alog"][k * 128:(k + 1) * 128, :]).then_inc(s, 16)
    ctx.emit("gpsimd", _wl)
    pvt, pv_tgt = emit_vec_loads(ctx, ["Dp"], scope)
    dpv = pvt["Dp"]
    ctx.wait("scalar", WL, wl_tgt)
    ac_a = ctx.sem("AC").inc(1)

    def _ae(sc, s=AC.h):
        for c in range(NCH):
            m = sc.activation(negA[c][:], alog[c][:], AF.Exp, bias=0.0, scale=1.0)
        m.then_inc(s, 1)
    ctx.emit("scalar", _ae)
    ctx.wait("vector", AC, ac_a)
    dv_a = DV.inc(1)

    def _an(v, s=DV.h):
        for c in range(NCH):
            v.tensor_scalar(negA[c][:], negA[c][:], -1.0, None, ALU.mult)
        for c in range(NCH):
            m = v.memset(hst[c][:], 0.0)
        m.then_inc(s, 1)
    ctx.emit("vector", _an)
    neg_ready = dv_a

    dlt = [ctx.sbuf(f"mb_dl_{c}", [128, TT], F32, scope) for c in range(NCH)]
    xct = [ctx.sbuf(f"mb_xc_{c}", [128, TT], F32, scope) for c in range(NCH)]
    szt = [ctx.sbuf(f"mb_sz_{c}", [128, TT], F32, scope) for c in range(NCH)]
    Pt = [ctx.sbuf(f"mb_P_{c}", [128, TT], F32, scope) for c in range(NCH)]
    Bn_sb = [ctx.sbuf(f"mb_Bn{j}", [1, TT], F32, scope) for j in range(2)]
    Cn_sb = [ctx.sbuf(f"mb_Cn{j}", [1, TT], F32, scope) for j in range(2)]
    Cb_sb = ctx.sbuf("mb_Cb", [128, TT], F32, scope)
    hcs = ctx.sbuf("mb_hcs", [128, 16 * TT], F32, scope)
    yt = [ctx.sbuf(f"mb_y_{c}", [128, TT], F32, scope) for c in range(NCH)]
    y2b = [ctx.sbuf(f"mb_y2_{c}", [128, TT], BF16, scope) for c in range(NCH)]
    da = [ctx.sbuf(f"mb_da_{j}", [128, TT], F32, scope) for j in range(2)]
    db = [ctx.sbuf(f"mb_db_{j}", [128, TT], F32, scope) for j in range(2)]
    hh = [ctx.sbuf(f"mb_hh_{j}", [128, TT], F32, scope) for j in range(2)]
    mo = [ctx.sbuf(f"mb_mo_{c}", [128, TT], F32, scope) for c in range(NCH)]
    pB = [ctx.psum(f"mb_pB{j}", [128, TT], F32, scope) for j in range(2)]
    pC = ctx.psum("mb_pC", [128, TT], F32, scope)
    pO = [ctx.psum(f"mb_pO{j}", [128, TT], F32, scope) for j in range(2)]

    ld_all = None
    st_done = None
    for i in range(NT):
        if i > 0:
            ctx.wait("sync", DV, DV.val)
            ctx.wait("sync", ctx.sem("PL"), PL.val)
        ld_tgt = LD.inc(16 * (NCH * 3))

        def _ld(sy, i=i, s=LD.h):
            for c in range(NCH):
                sl0 = slice(c * 128, (c + 1) * 128)
                sl1 = slice(i * TT, (i + 1) * TT)
                sy.dma_start(out=dlt[c][:], in_=ctx.dram["delta"][sl0, sl1]).then_inc(s, 16)
                sy.dma_start(out=xct[c][:], in_=ctx.dram["xc32"][sl0, sl1]).then_inc(s, 16)
                sy.dma_start(out=szt[c][:], in_=ctx.dram["sz"][sl0, sl1]).then_inc(s, 16)
        ctx.emit("sync", _ld)

        ctx.wait("vector", LD, ld_tgt)
        ctx.wait("vector", PE, PE.val)
        dv_p = DV.inc(1)

        def _P(v, s=DV.h):
            for c in range(NCH):
                m = v.tensor_tensor(Pt[c][:], dlt[c][:], xct[c][:], ALU.mult)
            m.then_inc(s, 1)
        ctx.emit("vector", _P)

        for c in range(NCH):
            eng = "vector"
            esem = DV
            for n in range(16):
                j = n % 2
                ctx.wait("sync", PE, PE.val)
                BL = ctx.sem("BL")
                bl_tgt = BL.inc(32)

                def _bl(sy, i=i, n=n, j=j, s=BL.h):
                    sy.dma_start(out=Bn_sb[j][:],
                                 in_=ctx.dram["Bm"][n:n + 1, i * TT:(i + 1) * TT]).then_inc(s, 16)
                    sy.dma_start(out=Cn_sb[j][:],
                                 in_=ctx.dram["Cm"][n:n + 1, i * TT:(i + 1) * TT]).then_inc(s, 16)
                ctx.emit("sync", _bl)
                ctx.wait("tensor", ctx.sem("BL"), bl_tgt)
                if i == 0 and c == 0 and n == 0:
                    ctx.wait("tensor", DV, neg_ready)
                pe_b = PE.inc(1)

                def _bc(t, n=n, j=j, s=PE.h):
                    t.matmul(pB[j][:], ctx.ones1[:], Bn_sb[j][:], start=True, stop=True)
                    t.matmul(pC[:], ctx.ones1[:], Cn_sb[j][:],
                             start=True, stop=True).then_inc(s, 1)
                ctx.emit("tensor", _bc)
                ctx.wait("scalar", PE, pe_b)
                ac_cb = AC.inc(1)

                def _cbe(sc, s=AC.h):
                    sc.activation(Cb_sb[:], pC[:], AF.Copy, bias=0.0, scale=1.0).then_inc(s, 1)
                ctx.emit("scalar", _cbe)
                ac_da = AC.inc(1)

                def _da(sc, c=c, n=n, j=j, s=AC.h):
                    sc.activation(da[j][:], dlt[c][:], AF.Exp, bias=0.0,
                                  scale=negA[c][:, n:n + 1]).then_inc(s, 1)
                ctx.emit("scalar", _da)
                ctx.wait("vector", PE, pe_b)
                ctx.wait("vector", DV, dv_p)
                dv_db = DV.inc(1)

                def _db(v, c=c, j=j, s=DV.h):
                    v.tensor_tensor(db[j][:], Pt[c][:], pB[j][:], ALU.mult).then_inc(s, 1)
                ctx.emit("vector", _db)
                ctx.wait(eng, AC, ac_da)
                ctx.wait(eng, AC, ac_cb)
                if eng == "gpsimd":
                    ctx.wait(eng, DV, dv_db)
                e_t = esem.inc(1)

                def _scan(x, c=c, n=n, j=j, s=esem.h):
                    x.tensor_tensor_scan(hh[j][:], da[j][:], db[j][:],
                                         hst[c][:, n:n + 1], ALU.mult, ALU.add)
                    x.tensor_copy(hst[c][:, n:n + 1], hh[j][:, TT - 1:TT])
                    x.tensor_tensor(hcs[:, n * TT:(n + 1) * TT], hh[j][:], Cb_sb[:],
                                    ALU.mult).then_inc(s, 1)
                ctx.emit(eng, _scan)
                ctx.wait("tensor", esem, e_t) if False else None
                if n >= 1:
                    pass
                ctx.wait("tensor", esem, e_t)
                ctx.wait("scalar", esem, e_t)
                ctx.wait("vector", esem, e_t) if eng == "gpsimd" else None
            ctx.wait("vector", esem, esem.val)
            e_y = DV.inc(1)

            def _yr(x, c=c, s=DV.h):
                x.tensor_reduce(yt[c][:], hcs[:].rearrange("p (n t) -> p t n", n=16),
                                AX.X, ALU.add).then_inc(s, 1)
            ctx.emit("vector", _yr)
            dv_y2 = DV.inc(1)

            def _y2(v, c=c, s=DV.h):
                v.scalar_tensor_tensor(yt[c][:], xct[c][:], dpv[:, c:c + 1], yt[c][:],
                                       ALU.mult, ALU.add)
                v.tensor_tensor(y2b[c][:], yt[c][:], szt[c][:], ALU.mult).then_inc(s, 1)
            ctx.emit("vector", _y2)

        y_all = DV.val
        for oc in range(NCH):
            j = oc % 2
            ctx.wait("tensor", DV, y_all)
            pe_o = PE.inc(1)

            def _om(t, oc=oc, j=j, s=PE.h):
                for k in range(NCH):
                    m = t.matmul(pO[j][:], opw[k][:, oc * 128:(oc + 1) * 128],
                                 y2b[k][:], start=(k == 0), stop=(k == NCH - 1))
                m.then_inc(s, 1)
            ctx.emit("tensor", _om)
            ctx.wait("scalar", PE, pe_o)
            if st_done is not None and oc == 0:
                ctx.wait("scalar", ST, st_done)
            ac_o = AC.inc(1)

            def _oe(sc, oc=oc, j=j, s=AC.h):
                sc.activation(mo[oc][:], pO[j][:], AF.Copy, bias=0.0,
                              scale=1.0).then_inc(s, 1)
            ctx.emit("scalar", _oe)
        ctx.wait("gpsimd", AC, AC.val)
        st_tgt = ST.inc(16 * NCH)
        st_done = st_tgt

        def _st(g, i=i, s=ST.h):
            for c in range(NCH):
                g.dma_start(out=ctx.dram["m_out"][c * 128:(c + 1) * 128,
                                                  i * TT:(i + 1) * TT],
                            in_=mo[c][:]).then_inc(s, 16)
        ctx.emit("gpsimd", _st)

    ctx.wait("sync", ST, ST.val)
    scope.close()



def emit_gating(ctx):
    emit_consts(ctx)
    scope = ExitStack()
    xn = ctx.dram["xnL1"]
    LD = ctx.sem("LD"); PE = ctx.sem("PE"); DV = ctx.sem("DV")
    AC = ctx.sem("AC"); ST = ctx.sem("ST"); WL = ctx.sem("WL")
    NEG = -1e30

    gw = [ctx.sbuf(f"g_w_{k}", [128, 32], F32, scope) for k in range(NCH)]
    gbr = ctx.sbuf("g_br", [1, 32], F32, scope)
    gb = ctx.sbuf("g_gb", [128, 32], F32, scope)
    wl_tgt = WL.inc(16 * (NCH + 1))

    def _wl(g, s=WL.h):
        for k in range(NCH):
            g.dma_start(out=gw[k][:],
                        in_=ctx.dram["gwT"][k * 128:(k + 1) * 128, :]).then_inc(s, 16)
        g.dma_start(out=gbr[:], in_=ctx.dram["gbias"][None, :]).then_inc(s, 16)
    ctx.emit("gpsimd", _wl)
    pGB = ctx.psum("g_pGB", [128, 32], F32, scope)
    ctx.wait("tensor", WL, wl_tgt)
    ctx.wait("tensor", DV, ctx.consts_ready)
    pe_gb = PE.inc(1)

    def _gbm(t, s=PE.h):
        t.matmul(pGB[:], ctx.ones1[:], gbr[:], start=True, stop=True).then_inc(s, 1)
    ctx.emit("tensor", _gbm)
    ctx.wait("vector", PE, pe_gb)
    dv_gb = DV.inc(1)

    def _gbe(v, s=DV.h):
        v.tensor_copy(gb[:], pGB[:]).then_inc(s, 1)
    ctx.emit("vector", _gbe)

    xf = [[ctx.sbuf(f"g_x{r}_{k}", [128, TT], F32, scope) for k in range(NCH)]
          for r in range(2)]
    W = [ctx.sbuf(f"g_wk{p}", [128, 256], F32, scope) for p in range(2)]
    W2 = [ctx.sbuf(f"g_w2{p}", [128, 128], F32, scope) for p in range(2)]
    comb = [ctx.sbuf(f"g_cb{p}", [128, 32], F32, scope) for p in range(2)]
    ctv = [ctx.sbuf(f"g_ct{p}", [32, 128], F32, scope) for p in range(2)]
    pL = [ctx.psum(f"g_pL{p}", [128, 32], F32, scope) for p in range(2)]
    pT = [ctx.psum(f"g_pT{p}", [32, 128], F32, scope) for p in range(2)]

    xf_free = [None, None]
    pL_free = [None, None]
    W_free = [None, None]
    pT_free = [None, None]
    comb_free = [None, None]
    ct_free = [None, None]
    it = 0
    for i in range(NT):
        r = i % 2
        if xf_free[r] is not None:
            ctx.wait("sync", PE, xf_free[r])
        ld_tgt = LD.inc(16 * NCH)

        def _ld(sy, r=r, i=i, s=LD.h):
            for k in range(NCH):
                sy.dma_start(out=xf[r][k][:],
                             in_=xn[k * 128:(k + 1) * 128, i * TT:(i + 1) * TT]).then_inc(s, 16)
        ctx.emit("sync", _ld)

        for tc in range(4):
            p = it % 2
            it += 1
            ctx.wait("tensor", LD, ld_tgt)
            if pL_free[p] is not None:
                ctx.wait("tensor", AC, pL_free[p])
            pe_l = PE.inc(1)

            def _lm(t, r=r, tc=tc, p=p, s=PE.h):
                for k in range(NCH):
                    m = t.matmul(pL[p][:], xf[r][k][:, tc * 128:(tc + 1) * 128],
                                 gw[k][:], start=(k == 0), stop=(k == NCH - 1))
                m.then_inc(s, 1)
            ctx.emit("tensor", _lm)
            ctx.wait("scalar", PE, pe_l)
            if W_free[p] is not None:
                ctx.wait("scalar", DV, W_free[p])
            ac_s = AC.inc(1)
            pL_free[p] = ac_s

            def _sg(sc, p=p, s=AC.h):
                sc.activation(W[p][:, 0:32], pL[p][:], AF.Sigmoid, bias=0.0,
                              scale=1.0).then_inc(s, 1)
            ctx.emit("scalar", _sg)
            ctx.wait("vector", AC, ac_s)
            if i == 0 and tc == 0:
                ctx.wait("vector", DV, dv_gb)
            if comb_free[p] is not None:
                ctx.wait("vector", PE, comb_free[p])
            nflush = 18
            base = DV.val
            DV.inc(nflush + 1)
            dv_done = base + nflush + 1
            W_free[p] = dv_done

            def _chain(v, p=p, s=DV.h, base=base):
                fl = [base + 1 + k for k in range(18)]
                w = W[p]
                w2 = W2[p]

                def FL(k, inst):
                    inst.then_inc(s, 1)
                    v.wait_ge(s, fl[k])
                FL(0, v.tensor_tensor(w[:, 32:64], w[:, 0:32], gb[:], ALU.add))
                pr = [(0, 1), (0, 2), (0, 3), (1, 2), (1, 3), (2, 3)]
                sc2 = w[:, 32:64].rearrange("p (g e) -> p g e", e=4)
                for jj, (a, b) in enumerate(pr):
                    m = v.tensor_tensor(w[:, 64 + jj * 8:64 + (jj + 1) * 8],
                                        sc2[:, :, a], sc2[:, :, b], ALU.add)
                FL(1, m)
                m = v.tensor_max(w[:, 112:120], w[:, 64:72], w[:, 72:80])
                m = v.tensor_max(w[:, 120:128], w[:, 80:88], w[:, 88:96])
                m = v.tensor_max(w[:, 128:136], w[:, 96:104], w[:, 104:112])
                FL(2, m)
                m = v.tensor_max(w[:, 136:144], w[:, 112:120], w[:, 120:128])
                FL(3, m)
                m = v.tensor_max(w[:, 144:152], w[:, 136:144], w[:, 128:136])
                FL(4, m)
                FL(5, v.max(out=w[:, 152:160], in_=w[:, 144:152]))
                FL(6, v.memset(w[:, 156:160], NEG))
                FL(7, v.match_replace(out=w[:, 160:168], in_to_replace=w[:, 152:160],
                                      in_values=w[:, 144:152], imm_value=NEG))
                FL(8, v.tensor_tensor(w[:, 168:176], w[:, 144:152], w[:, 160:168],
                                      ALU.not_equal))
                FL(9, v.tensor_copy(w[:, 176:208].rearrange("p (g e) -> p g e", e=4),
                                    w[:, 168:176].rearrange("p (g o) -> p g o", o=1)
                                    .broadcast_to([128, 8, 4])))
                FL(10, v.tensor_tensor(w[:, 208:240], w[:, 32:64], w[:, 176:208],
                                       ALU.mult))
                FL(11, v.max(out=w2[:, 0:8], in_=w[:, 208:240]))
                FL(12, v.match_replace(out=w2[:, 8:40], in_to_replace=w2[:, 0:8],
                                       in_values=w[:, 208:240], imm_value=NEG))
                FL(13, v.tensor_tensor(w2[:, 40:72], w[:, 208:240], w2[:, 8:40],
                                       ALU.not_equal))
                FL(14, v.tensor_tensor(w2[:, 72:104], w[:, 0:32], w2[:, 40:72],
                                       ALU.mult))
                FL(15, v.tensor_reduce(w2[:, 104:105], w2[:, 72:104], AX.X, ALU.add))
                FL(16, v.tensor_scalar(w2[:, 105:106], w2[:, 104:105], 1e-20, None,
                                       ALU.add))
                m = v.reciprocal(w2[:, 106:107], w2[:, 105:106])
                m.then_inc(s, 1)
                v.wait_ge(s, fl[17])
                v.tensor_scalar(comb[p][:], w2[:, 72:104], w2[:, 106:107], 2.5,
                                ALU.mult, ALU.mult).then_inc(s, 1)
            ctx.emit("vector", _chain)
            if i == NT - 1 and tc == 3:
                xf_free[r] = PE.val
            ctx.wait("tensor", DV, dv_done)
            if pT_free[p] is not None:
                ctx.wait("tensor", DV, pT_free[p])
            pe_t = PE.inc(1)
            comb_free[p] = pe_t

            def _tp(t, p=p, s=PE.h, ctx=ctx):
                t.transpose(pT[p][:], comb[p][:], ctx.identf[:]).then_inc(s, 1)
            ctx.emit("tensor", _tp)
            ctx.wait("vector", PE, pe_t)
            if ct_free[p] is not None:
                ctx.wait("vector", ST, ct_free[p])
            dv_ct = DV.inc(1)
            pT_free[p] = dv_ct

            def _cte(v, p=p, s=DV.h):
                v.tensor_copy(ctv[p][:], pT[p][:]).then_inc(s, 1)
            ctx.emit("vector", _cte)
            ctx.wait("gpsimd", DV, dv_ct)
            st_tgt = ST.inc(16)
            ct_free[p] = st_tgt

            def _st(g, i=i, tc=tc, p=p, s=ST.h):
                g.dma_start(out=ctx.dram["cT"][:, i * TT + tc * 128:i * TT + (tc + 1) * 128],
                            in_=ctv[p][:]).then_inc(s, 16)
            ctx.emit("gpsimd", _st)
        xf_free[r] = PE.val

    ctx.wait("sync", ST, ST.val)
    scope.close()


def emit_moe(ctx):
    emit_consts(ctx)
    scope = ExitStack()
    xnb = ctx.dram["xnL1b"]
    LD = ctx.sem("LD"); PE = ctx.sem("PE"); DV = ctx.sem("DV")
    AC = ctx.sem("AC"); ST = ctx.sem("ST"); WL = ctx.sem("WL")
    PL = ctx.sem("PL"); BL = ctx.sem("BL")
    QT = 1024
    NQ = T // QT

    xq = [ctx.sbuf(f"e_x_{k}", [128, QT], BF16, scope) for k in range(NCH)]
    yq = [ctx.sbuf(f"e_y_{c}", [128, QT], F32, scope) for c in range(NCH)]
    wg = [[ctx.sbuf(f"e_wg{j}_{k}", [128, 512], BF16, scope) for k in range(NCH)]
          for j in range(2)]
    wu = [[ctx.sbuf(f"e_wu{j}_{k}", [128, 512], BF16, scope) for k in range(NCH)]
          for j in range(2)]
    wd = [[ctx.sbuf(f"e_wd{ei}_{k}", [128, D], BF16, scope) for k in range(4)]
          for ei in range(4)]
    ce = [ctx.sbuf(f"e_ce{j}", [1, QT], F32, scope) for j in range(2)]
    cb = [ctx.sbuf(f"e_cb{j}", [128, TT], BF16, scope) for j in range(2)]
    sg = [ctx.sbuf(f"e_sg_{ic}", [128, TT], BF16, scope) for ic in range(4)]
    cs = [ctx.sbuf(f"e_cs_{ic}", [128, TT], BF16, scope) for ic in range(4)]
    gu = [[[ctx.sbuf(f"e_gu{ei}_{tt}_{ic}", [128, TT], BF16, scope)
            for ic in range(4)] for tt in range(2)] for ei in range(4)]
    pb = [ctx.psum(f"e_pb{b}", [128, TT], F32, scope) for b in range(8)]

    y_st = None
    wgu_free = [None, None]
    wd_free = [None] * 4
    ce_free = [None, None]
    cb_free = [None, None]
    sg_free = None
    cs_free = None
    gu_free = None
    pbf = [None] * 8
    for q in range(NQ):
        if y_st is not None:
            ctx.wait("sync", ST, y_st)
        if q > 0:
            ctx.wait("sync", PE, PE.val)
        ld_tgt = LD.inc(16 * NCH * 2)

        def _ld(sy, q=q, s=LD.h):
            for c in range(NCH):
                sy.dma_start(out=yq[c][:],
                             in_=ctx.dram["base"][c * 128:(c + 1) * 128,
                                                  q * QT:(q + 1) * QT]).then_inc(s, 16)
            for k in range(NCH):
                sy.dma_start(out=xq[k][:],
                             in_=xnb[k * 128:(k + 1) * 128,
                                     q * QT:(q + 1) * QT]).then_inc(s, 16)
        ctx.emit("sync", _ld)

        for grp in range(8):
            for ei in range(4):
                e = grp * 4 + ei
                j = ei % 2
                if wgu_free[j] is not None:
                    ctx.wait("gpsimd", PE, wgu_free[j])
                if wd_free[ei] is not None:
                    ctx.wait("gpsimd", PE, wd_free[ei])
                wl_tgt = WL.inc(16 * (NCH + NCH + 4))

                def _wl(g, e=e, j=j, ei=ei, s=WL.h):
                    for k in range(NCH):
                        g.dma_start(out=wg[j][k][:],
                                    in_=ctx.dram["wgT"][e, k * 128:(k + 1) * 128, :]).then_inc(s, 16)
                    for k in range(NCH):
                        g.dma_start(out=wu[j][k][:],
                                    in_=ctx.dram["wuT"][e, k * 128:(k + 1) * 128, :]).then_inc(s, 16)
                    for k in range(4):
                        g.dma_start(out=wd[ei][k][:],
                                    in_=ctx.dram["wdT"][e, k * 128:(k + 1) * 128, :]).then_inc(s, 16)
                ctx.emit("gpsimd", _wl)
                if ce_free[j] is not None:
                    ctx.wait("sync", PE, ce_free[j])
                bl_tgt = BL.inc(16)

                def _cl(sy, e=e, q=q, j=j, s=BL.h):
                    sy.dma_start(out=ce[j][:],
                                 in_=ctx.dram["cT"][e:e + 1, q * QT:(q + 1) * QT]).then_inc(s, 16)
                ctx.emit("sync", _cl)

                for tt in range(2):
                    jt = tt
                    ctx.wait("tensor", BL, bl_tgt)
                    if q == 0 and grp == 0 and ei == 0 and tt == 0:
                        ctx.wait("tensor", DV, ctx.consts_ready)
                        ctx.wait("tensor", LD, ld_tgt)
                    if pbf[0] is not None:
                        ctx.wait("tensor", pbf[0][0], pbf[0][1])
                    pe_cb = PE.inc(1)
                    ce_free[j] = pe_cb

                    def _cbm(t, j=j, tt=tt, s=PE.h):
                        t.matmul(pb[0][:], ctx.ones1[:], ce[j][0:1, tt * TT:(tt + 1) * TT],
                                 start=True, stop=True).then_inc(s, 1)
                    ctx.emit("tensor", _cbm)
                    ctx.wait("scalar", PE, pe_cb)
                    if cb_free[jt] is not None:
                        ctx.wait("scalar", PL, cb_free[jt])
                    ac_cb = AC.inc(1)
                    pbf[0] = (AC, ac_cb)

                    def _cbe(sc, jt=jt, s=AC.h):
                        sc.activation(cb[jt][:], pb[0][:], AF.Copy, bias=0.0,
                                      scale=1.0).then_inc(s, 1)
                    ctx.emit("scalar", _cbe)
                    for b in range(4):
                        if pbf[b] is not None:
                            ctx.wait("tensor", pbf[b][0], pbf[b][1])
                            pbf[b] = None
                    ctx.wait("tensor", WL, wl_tgt)
                    if sg_free is not None:
                        ctx.wait("tensor", PL, sg_free)
                    pe_g = PE.inc(1)

                    def _gm(t, j=j, tt=tt, s=PE.h):
                        for ic in range(4):
                            for k in range(NCH):
                                m = t.matmul(pb[ic][:],
                                             wg[j][k][:, ic * 128:(ic + 1) * 128],
                                             xq[k][:, tt * TT:(tt + 1) * TT],
                                             start=(k == 0), stop=(k == NCH - 1))
                        m.then_inc(s, 1)
                    ctx.emit("tensor", _gm)
                    ctx.wait("scalar", PE, pe_g)
                    ac_sg = AC.inc(1)
                    for b in range(4):
                        pbf[b] = (AC, ac_sg)

                    def _sge(sc, s=AC.h):
                        for ic in range(4):
                            m = sc.activation(sg[ic][:], pb[ic][:], AF.Silu,
                                              bias=0.0, scale=1.0)
                        m.then_inc(s, 1)
                    ctx.emit("scalar", _sge)
                    for b in range(4, 8):
                        if pbf[b] is not None:
                            ctx.wait("tensor", pbf[b][0], pbf[b][1])
                            pbf[b] = None
                    if gu_free is not None and ei == 0 and tt == 0:
                        ctx.wait("vector", PE, gu_free)
                    pe_u = PE.inc(1)

                    def _um(t, j=j, tt=tt, s=PE.h):
                        for ic in range(4):
                            for k in range(NCH):
                                m = t.matmul(pb[4 + ic][:],
                                             wu[j][k][:, ic * 128:(ic + 1) * 128],
                                             xq[k][:, tt * TT:(tt + 1) * TT],
                                             start=(k == 0), stop=(k == NCH - 1))
                        m.then_inc(s, 1)
                    ctx.emit("tensor", _um)
                    ctx.wait("gpsimd", AC, ac_sg)
                    ctx.wait("gpsimd", AC, ac_cb)
                    if cs_free is not None:
                        ctx.wait("gpsimd", DV, cs_free)
                    pl_cs = PL.inc(1)
                    cb_free[jt] = pl_cs
                    sg_free = pl_cs

                    def _cs(g, jt=jt, s=PL.h):
                        for ic in range(4):
                            m = g.tensor_tensor(cs[ic][:], sg[ic][:], cb[jt][:], ALU.mult)
                        m.then_inc(s, 1)
                    ctx.emit("gpsimd", _cs)
                    ctx.wait("vector", PE, pe_u)
                    ctx.wait("vector", PL, pl_cs)
                    dv_gu = DV.inc(1)
                    cs_free = dv_gu
                    for b in range(4, 8):
                        pbf[b] = (DV, dv_gu)

                    def _gu(v, ei=ei, tt=tt, s=DV.h):
                        for ic in range(4):
                            m = v.tensor_tensor(gu[ei][tt][ic][:], cs[ic][:],
                                                pb[4 + ic][:], ALU.mult)
                        m.then_inc(s, 1)
                    ctx.emit("vector", _gu)
                wgu_free[j] = PE.val
            gu_all = DV.val
            for tt in range(2):
                for oc in range(NCH):
                    ctx.wait("tensor", DV, gu_all)
                    if q == 0 and grp == 0 and tt == 0 and oc == 0:
                        ctx.wait("tensor", LD, ld_tgt)
                    if pbf[oc] is not None:
                        ctx.wait("tensor", pbf[oc][0], pbf[oc][1])
                        pbf[oc] = None
                    pe_d = PE.inc(1)

                    def _dm(t, oc=oc, tt=tt, s=PE.h):
                        first = True
                        for ei in range(4):
                            for ic in range(4):
                                m = t.matmul(pb[oc][:],
                                             wd[ei][ic][:, oc * 128:(oc + 1) * 128],
                                             gu[ei][tt][ic][:],
                                             start=first, stop=(ei == 3 and ic == 3))
                                first = False
                        m.then_inc(s, 1)
                    ctx.emit("tensor", _dm)
                    ctx.wait("vector", PE, pe_d)
                    if q == 0 and grp == 0 and tt == 0 and oc == 0:
                        ctx.wait("vector", LD, ld_tgt)
                    dv_y = DV.inc(1)
                    pbf[oc] = (DV, dv_y)

                    def _ye(v, oc=oc, tt=tt, s=DV.h):
                        v.tensor_tensor(yq[oc][:, tt * TT:(tt + 1) * TT],
                                        yq[oc][:, tt * TT:(tt + 1) * TT],
                                        pb[oc][:], ALU.add).then_inc(s, 1)
                    ctx.emit("vector", _ye)
            gu_free = PE.val
            for ei in range(4):
                wd_free[ei] = PE.val
        ctx.wait("gpsimd", DV, DV.val)
        y_st = ST.inc(16 * NCH)

        def _st(g, q=q, s=ST.h):
            for c in range(NCH):
                g.dma_start(out=ctx.dram["xout"][c * 128:(c + 1) * 128,
                                                 q * QT:(q + 1) * QT],
                            in_=yq[c][:]).then_inc(s, 16)
        ctx.emit("gpsimd", _st)

    ctx.wait("sync", ST, ST.val)
    scope.close()


def build_program(phase_list, ins, outs):
    nc = bass.Bass("TRN2", target_bir_lowering=False)
    ctx = Ctx(nc)
    all_specs = dict(WEIGHT_SPECS)
    all_specs.update(SCRATCH_SPECS)
    all_specs["x0"] = ([D, T], F32)
    all_specs["xout"] = ([D, T], F32)
    for nm in ins:
        shape, dt = all_specs[nm]
        ctx.dram[nm] = nc.dram_tensor(nm, shape, dt, kind="ExternalInput").ap()
    for nm in outs:
        shape, dt = all_specs[nm]
        ctx.dram[nm] = nc.dram_tensor(nm, shape, dt, kind="ExternalOutput").ap()
    for fn, kw in phase_list:
        fn(ctx, **kw)
    ctx.finalize()
    return nc


FUSED = True


def build_fused():
    nc = bass.Bass("TRN2", target_bir_lowering=False)
    ctx = Ctx(nc)
    ctx.dram["x0"] = nc.dram_tensor("x0", [D, T], F32, kind="ExternalInput").ap()
    ctx.dram["xout"] = nc.dram_tensor("xout", [D, T], F32, kind="ExternalOutput").ap()
    for nm, (shape, dt_) in WEIGHT_SPECS.items():
        ctx.dram[nm] = nc.dram_tensor(nm, shape, dt_, kind="ExternalInput").ap()
    for nm, (shape, dt_) in SCRATCH_SPECS.items():
        if nm.startswith("dbg_"):
            continue
        ctx.dram[nm] = nc.dram_tensor(nm, shape, dt_, kind="Internal").ap()
    pi = 0
    for (phases, ins, outs) in _launches():
        for fn, kw in phases:
            ctx.phase_tag = f"P{pi}_"
            pi += 1
            fn(ctx, **kw)
        ctx.barrier()
    ctx.finalize()
    return nc


def kernel(**inputs):
    w, xs = host_prep(inputs)
    if FUSED:
        nc = build_fused()
        in_maps = []
        for i in range(NCORE):
            m = {"x0": xs[i]}
            m.update(w)
            in_maps.append(m)
        res = run_bass_kernel_spmd(nc, in_maps, core_ids=list(range(NCORE)))
        out = np.empty((NCORE, T, D), np.float32)
        for i in range(NCORE):
            out[i] = np.asarray(res.results[i]["xout"]).T
        return out

    vals = {}
    for i in range(NCORE):
        vals[i] = {"x0": xs[i]}

    launches = _launches()
    for (phases, ins, outs) in launches:
        nc = build_program(phases, ins, outs)
        in_maps = []
        for i in range(NCORE):
            m = {}
            for nm in ins:
                m[nm] = vals[i][nm] if nm in vals[i] else w[nm]
            in_maps.append(m)
        res = run_bass_kernel_spmd(nc, in_maps, core_ids=list(range(NCORE)))
        for i in range(NCORE):
            for nm in outs:
                vals[i][nm] = np.asarray(res.results[i][nm])

    out = np.empty((NCORE, T, D), np.float32)
    for i in range(NCORE):
        out[i] = vals[i]["xout"].T
    return out


def _launches():
    return [
        ([(emit_ln, dict(src_name="x0", outs=[("xn0", BF16)]))],
         ["x0"], ["xn0"]),
        ([(emit_l0_attn, dict())],
         ["xn0", "qwT", "kkwT", "vwT", "projT", "proj_b"], ["att"]),
        ([(emit_l0_ffn, dict())],
         ["xn0", "x0", "att", "g1R", "g2R", "g1I", "g2I", "l0f_br", "l0f_bi"],
         ["x1", "wtR", "wtI"]),
        ([(emit_ln, dict(src_name="x1", outs=[("xnL1", F32), ("xnL1b", BF16)]))],
         ["x1"], ["xnL1", "xnL1b"]),
        ([(emit_ln, dict(src_name="xnL1", outs=[("s1", BF16)], affine=("n1_g", "n1_b")))],
         ["xnL1", "n1_g", "n1_b"], ["s1"]),
        ([(emit_mamba_a, dict())],
         ["s1", "inpT", "convw0", "convw1", "convb", "xprojT", "dtprojT", "dtprojb"],
         ["xc32", "xcb", "sz", "delta", "Bm", "Cm"]),
        ([(emit_mamba_b, dict())],
         ["xc32", "delta", "Bm", "Cm", "sz", "outprojT", "Alog", "Dp"], ["m_out"]),
        ([(emit_ln, dict(src_name="m_out", outs=[("src2", F32)],
                         affine=("mnorm_g", "mnorm_b"), res_name="xnL1"))],
         ["m_out", "xnL1", "mnorm_g", "mnorm_b"], ["src2"]),
        ([(emit_ln, dict(src_name="src2", outs=[("s2", BF16)], affine=("n2_g", "n2_b")))],
         ["src2", "n2_g", "n2_b"], ["s2"]),
        ([(emit_matmul, dict(src_name="s2", w_name="ffw1T", K=D, outF=4 * D,
                             outs=[("f1", BF16)], lrelu=True))],
         ["s2", "ffw1T"], ["f1"]),
        ([(emit_matmul, dict(src_name="f1", w_name="ffw2T", K=4 * D, outF=D,
                             outs=[("src3", F32)], adds=("src2",)))],
         ["f1", "ffw2T", "src2"], ["src3"]),
        ([(emit_ln, dict(src_name="src3", outs=[("s3", BF16)], affine=("n3_g", "n3_b")))],
         ["src3", "n3_g", "n3_b"], ["s3"]),
        ([(emit_matmul, dict(src_name="s3", w_name="feT", K=D, outF=D,
                             outs=[("a_out", F32)], bias_name="feb", adds=("src3",)))],
         ["s3", "feT", "feb", "src3"], ["a_out"]),
        ([(emit_gating, dict())],
         ["xnL1", "gwT", "gbias"], ["cT"]),
        ([(emit_shared, dict())],
         ["xnL1b", "x1", "a_out", "shgT", "shuT", "shdT"], ["base"]),
        ([(emit_moe, dict())],
         ["xnL1b", "cT", "base", "wgT", "wuT", "wdT"], ["xout"]),
    ]


if __name__ == "__main__":
    pass
